# version 02 (final)
# baseline (speedup 1.0000x reference)
import numpy as np
import concourse.bass as bass
import concourse.mybir as mybir
from concourse.bass_utils import run_bass_kernel_spmd
from contextlib import ExitStack

F32 = mybir.dt.float32
BF16 = mybir.dt.bfloat16
AL = mybir.AluOpType
AF = mybir.ActivationFunctionType
AX = mybir.AxisListType.X

B = 2
T = 2048
NTOK = B * T
DC = 8
HPAIRS = 8
CAP = 4096
NCORES = 8
SEARCH_ITERS = 45
D = 1024

STAGE = 6


class Builder:
    def __init__(self, nc, stage, dumps):
        self.nc = nc
        self.stage = stage
        self.dumps = dumps
        self.ctx = ExitStack()
        self.sched = None
        self.bank_free = {}
        self.dump_aps = []

    def sb(self, name, shape, dt, ctx=None):
        return (ctx or self.ctx).enter_context(self.nc.sbuf_tensor(name, shape, dt))

    def psum(self, name, shape, dt):
        return self.ctx.enter_context(self.nc.psum_tensor(name, shape, dt))

    def sem(self, name):
        class Ct:
            __slots__ = ("sem", "v")
        c = Ct()
        c.sem = self.ctx.enter_context(self.nc.semaphore(name))
        c.v = 0
        return c

    def new_sched(self):
        self.sched = {k: [] for k in ("tensor", "vector", "scalar", "sync", "gpsimd")}

    def run_block(self, name):
        s = self.sched
        with self.nc.Block(name) as blk:
            for eng in ("gpsimd", "sync", "scalar", "vector", "tensor"):
                if s[eng]:
                    getattr(blk, eng)(
                        (lambda ops: lambda e: [f(e) for f in ops])(tuple(s[eng])))

    def em(self, eng, f):
        self.sched[eng].append(f)

    def emi(self, eng, f, ct, n=1):
        ct.v += n
        self.sched[eng].append(lambda e, _f=f, _s=ct.sem, _n=n: _f(e).then_inc(_s, _n))

    def wait(self, eng, ct, v=None):
        vv = ct.v if v is None else v
        if vv <= 0:
            return
        self.sched[eng].append(lambda e, _s=ct.sem, _v=vv: e.wait_ge(_s, _v))

    def vfill(self):
        self.sched["vector"].append(
            lambda v: v.tensor_copy(self.vjunk[:, 0:32], self.vjunk[:, 32:64]))

    def te_bank(self, pobj):
        g = self.bank_free.get(id(pobj))
        if g is not None:
            self.wait("tensor", g[0], g[1])

    def consumed(self, pobj, ct):
        self.bank_free[id(pobj)] = (ct, ct.v)

    def build(self):
        nc = self.nc
        stage = self.stage

        self.x_d = nc.dram_tensor("x", [NTOK, D], F32, kind="ExternalInput")
        self.ipw_d = nc.dram_tensor("ipw", [128, 26 * 1024], F32, kind="ExternalInput")
        self.convw_d = nc.dram_tensor("convw", [128, 96], F32, kind="ExternalInput")
        self.convb_d = nc.dram_tensor("convb", [128, 24], F32, kind="ExternalInput")
        self.wbase_d = nc.dram_tensor("wbase", [16, 1], F32, kind="ExternalInput")
        self.opw_d = nc.dram_tensor("opw", [128, 8192], F32, kind="ExternalInput")
        self.w1_d = nc.dram_tensor("w1", [8, 128, 8192], F32, kind="ExternalInput")
        self.w2_d = nc.dram_tensor("w2", [8, 128, 8192], F32, kind="ExternalInput")
        self.out_d = nc.dram_tensor("out", [NTOK, D], F32, kind="ExternalOutput")
        self.cc_in = nc.dram_tensor("cc_in", [8, NTOK], F32)
        self.cc_out = nc.dram_tensor("cc_out", [NCORES * 8, NTOK], F32)

        self.xn = self.sb("xn", [128, DC, NTOK], BF16)
        self.ya = self.sb("ya", [128, DC, NTOK], BF16)
        self.sc_tm = self.sb("sc_tm", [128, 32, 8], F32)
        self.idbf = self.sb("idbf", [128, 128], BF16)
        self.idf32 = self.sb("idf32", [128, 128], F32)
        self.trimask = self.sb("trimask", [128, 64], BF16)
        self.ones1 = self.sb("ones1", [1, 128], F32)
        self.wbase = self.sb("wbase_s", [16, 1], F32)
        self.convw = self.sb("convw_s", [128, 96], F32)
        self.convb = self.sb("convb_s", [128, 24], F32)
        self.zcol = self.sb("zcol", [128, 1], F32)
        self.epsc = self.sb("epsc", [128, 1], F32)
        self.neg1 = self.sb("neg1", [128, 1], F32)
        self.vjunk = self.sb("vjunk", [128, 64], BF16)

        self.P = [self.psum(f"P{i}", [128, 512], F32) for i in range(6)]
        self.PT = self.psum("PT", [128, 1024], BF16)
        self.PG = self.psum("PG", [128, 512], F32)

        self.sX = self.sem("sX")
        self.sW = self.sem("sW")
        self.sTE = self.sem("sTE")
        self.sVE = self.sem("sVE")
        self.sSE = self.sem("sSE")
        self.sCC = self.sem("sCC")
        self.sOUT = self.sem("sOUT")

        self.phase_ln()
        if stage >= 2:
            self.phase_attn()
        if stage >= 6:
            self.phase_ffn()
        self.phase_dumps()
        if getattr(self, "actx_open", None) is not None:
            self.actx_open.close()
        self.ctx.close()
        return nc

    def phase_ln(self):
        nc = self.nc
        self.new_sched()
        em, emi, wait = self.em, self.emi, self.wait
        sX, sW, sTE, sVE, sSE = self.sX, self.sW, self.sTE, self.sVE, self.sSE
        idbf, idf32, trimask, ones1, zcol = (
            self.idbf, self.idf32, self.trimask, self.ones1, self.zcol)
        PT = self.PT
        xn = self.xn

        def g_consts(g):
            g.memset(idbf[:, :], 0.0)
            g.affine_select(out=idbf[:, :], in_=idbf[:, :], compare_op=AL.not_equal,
                            fill=1.0, base=0, pattern=[[-1, 128]], channel_multiplier=1)
            g.memset(idf32[:, :], 0.0)
            g.affine_select(out=idf32[:, :], in_=idf32[:, :], compare_op=AL.not_equal,
                            fill=1.0, base=0, pattern=[[-1, 128]], channel_multiplier=1)
            g.memset(trimask[:, :], 1.0)
            g.affine_select(out=trimask[0:64, :], in_=trimask[0:64, :],
                            compare_op=AL.is_ge, fill=0.0, base=0,
                            pattern=[[1, 64]], channel_multiplier=-1)
            g.affine_select(out=trimask[64:128, :], in_=trimask[64:128, :],
                            compare_op=AL.is_ge, fill=0.0, base=0,
                            pattern=[[1, 64]], channel_multiplier=-1)
            g.memset(ones1[:, :], 1.0)
            g.memset(zcol[:, :], 0.0)
            g.memset(self.epsc[:, :], 1e-5)
            g.memset(self.neg1[:, :], -1.0)
            g.dma_start(out=self.wbase[:, :], in_=self.wbase_d.ap()).then_inc(
                sW.sem, 16)
            g.dma_start(out=self.convw[:, :], in_=self.convw_d.ap()).then_inc(
                sW.sem, 16)
            return g.dma_start(out=self.convb[:, :], in_=self.convb_d.ap())
        emi("gpsimd", g_consts, sW, 16)
        sW.v += 32
        self.W_SMALL = sW.v

        lnctx = ExitStack()
        bn2all = self.sb("bn2all", [128, 32, 8], F32,
                         self.ctx if "bn2" in self.dumps else lnctx)
        xt = self.sb("xt", [128, 2, 1024], F32, lnctx)
        xsq = self.sb("xsq", [128, 1024], F32, lnctx)
        xn_tm = self.sb("xn_tm", [128, 2, 1024], BF16, lnctx)
        if not hasattr(self, "attn_handles"):
            self.attn_handles = {}
        self.attn_handles["bn2"] = bn2all

        wait("tensor", sW, self.W_SMALL)
        wait("scalar", sW, self.W_SMALL)
        wait("vector", sW, self.W_SMALL)
        xt_free = [0, 0]
        pt_free = 0
        for tt in range(32):
            pp = tt % 2
            b2 = bn2all[:, tt, :]
            wait("sync", sVE, xt_free[pp])
            emi("sync", lambda s, _pp=pp, _tt=tt: s.dma_start(
                out=xt[:, _pp, :], in_=self.x_d[_tt * 128:(_tt + 1) * 128, :]), sX, 16)
            wait("vector", sX, 16 * (tt + 1))
            wait("vector", sTE, pt_free)
            em("vector", lambda v, _pp=pp, _b=b2: v.tensor_scalar(
                out=xn_tm[:, _pp, :], in0=xt[:, _pp, :], scalar1=1.0 / 1024,
                scalar2=None, op0=AL.mult, op1=AL.add, accum_out=_b[:, 0:1]))
            em("vector", lambda v, _pp=pp: v.tensor_tensor(
                xsq[:, :], xt[:, _pp, :], xt[:, _pp, :], AL.mult))
            em("vector", lambda v, _pp=pp, _b=b2: v.tensor_scalar(
                out=xn_tm[:, _pp, :], in0=xsq[:, :], scalar1=1.0 / 1024,
                scalar2=None, op0=AL.mult, op1=AL.add, accum_out=_b[:, 1:2]))
            em("vector", lambda v, _b=b2: v.tensor_tensor(_b[:, 4:5], _b[:, 0:1],
                                                          _b[:, 0:1], AL.mult))
            self.vfill()
            emi("vector", lambda v, _b=b2: v.tensor_tensor(_b[:, 1:2], _b[:, 1:2],
                                                           _b[:, 4:5], AL.subtract), sVE)
            wait("scalar", sVE)
            emi("scalar", lambda s, _b=b2: s.activation(_b[:, 2:3], _b[:, 1:2], AF.Sqrt,
                                                        bias=self.epsc[:, 0:1]), sSE)
            wait("vector", sSE)
            em("vector", lambda v, _b=b2: v.reciprocal(_b[:, 3:4], _b[:, 2:3]))
            self.vfill()
            emi("vector", lambda v, _pp=pp, _b=b2: v.tensor_scalar(
                out=xn_tm[:, _pp, :], in0=xt[:, _pp, :], scalar1=_b[:, 0:1],
                scalar2=_b[:, 3:4], op0=AL.subtract, op1=AL.mult), sVE)
            xt_free[pp] = sVE.v
            wait("tensor", sVE)
            self.te_bank(PT)
            def t_tr(t, _pp=pp):
                r = None
                for dc in range(8):
                    r = t.transpose(PT[:, dc * 128:(dc + 1) * 128],
                                    xn_tm[:, _pp, dc * 128:(dc + 1) * 128], idbf[:, :])
                return r
            emi("tensor", t_tr, sTE)
            pt_free = sTE.v
            wait("vector", sTE)
            def v_ev(v, _tt=tt):
                r = None
                for dc in range(8):
                    r = v.tensor_copy(xn[:, dc, _tt * 128:(_tt + 1) * 128],
                                      PT[:, dc * 128:(dc + 1) * 128])
                return r
            emi("vector", v_ev, sVE)
            self.consumed(PT, sVE)

        self.run_block("ln")
        lnctx.close()

    def phase_attn(self):
        nc = self.nc
        stage = self.stage
        self.new_sched()
        em, emi, wait = self.em, self.emi, self.wait
        sX, sW, sTE, sVE, sSE, sCC = (self.sX, self.sW, self.sTE, self.sVE,
                                      self.sSE, self.sCC)
        P, PT, PG = self.P, self.PT, self.PG
        xn, ya = self.xn, self.ya
        idbf, idf32 = self.idbf, self.idf32
        zcol = self.zcol

        scol = self.sb("scol", [64, 8], F32)
        scnt = self.sb("scnt", [64, 8], F32)
        lo8 = self.sb("lo8", [1, 8], F32)
        thr8 = self.sb("thr8", [128, 8], F32)
        actx = ExitStack()
        self.actx = actx
        ena_tm = self.sb("ena_tm", [128, 512], F32, actx)
        erm_tm = self.sb("erm_tm", [128, 512], F32, actx)
        sdo_tm = self.sb("sdo_tm", [128, 512], F32, actx)
        dsp_tm = self.sb("dsp_tm", [128, 512], F32, actx)
        norm_tm = self.sb("norm_tm", [128, 512], F32, actx)
        ydsc = self.sb("ydsc", [128, 512], F32, actx)
        yosc = self.sb("yosc", [128, 512], F32, actx)
        rta = self.sb("rta", [128, 5, 256], F32, actx)
        nse = self.sb("nse", [128, 32], F32, actx)
        stg8 = self.sb("stg8", [8, 512], F32, actx)
        r_all = self.sb("r_all", [64, 4096], F32, actx)
        auxctx = ExitStack()
        af1 = self.sb("af1", [16, 2048], F32, auxctx)
        af2 = self.sb("af2", [16, 2048], F32, auxctx)
        asm = self.sb("asm", [16, 256], F32, auxctx)
        auxctx.close()
        glog = stg8
        self.attn_handles = dict(ena_tm=ena_tm, erm_tm=erm_tm, sdo_tm=sdo_tm,
                                 dsp_tm=dsp_tm, norm_tm=norm_tm,
                                 ydsc=ydsc, yosc=yosc, rta=rta, r_all=r_all,
                                 scol=scol, thr8=thr8, scnt=scnt, lo8=lo8,
                                 af1=af1, af2=af2, asm=asm, nse=nse)

        hpctx = ExitStack()
        spu = self.sb("spu", [128, 2, 17, 64], BF16, hpctx)
        yts = self.sb("yts", [128, 16, 2, 64], BF16, hpctx)
        mt = self.sb("mt", [128, 16, 2, 64], BF16, hpctx)
        xtm = self.sb("xtm", [128, 16, 128], BF16, hpctx)
        instat = self.sb("instat", [128, 3, 8, 128], BF16, hpctx)
        strip = self.sb("strip", [128, 3, 524], BF16, hpctx)
        carry = self.sb("carry", [128, 3, 4], BF16, hpctx)
        diag = self.sb("diag", [128, 3, 4, 128], BF16, hpctx)
        cfm = self.sb("cfm", [128, 2048], BF16, hpctx)
        bfm = self.sb("bfm", [128, 512], BF16, hpctx)
        shx = self.sb("shx", [128, 512], BF16, hpctx)
        btm = self.sb("btm", [128, 4, 128], BF16, hpctx)
        self.attn_handles.update(cfm=cfm, xtm=xtm, spu=spu, mt=mt,
                                 yts=yts, btm=btm, bfm=bfm, shx=shx, strip=strip)

        AS = {k: slice(i * 32, (i + 1) * 32) for i, k in enumerate(
            ("csa", "rmp", "cmin", "cmax", "rs63", "ndp", "ndc", "tmp"))}

        oc_width = {24: 16, 25: 8}
        instat_ready = {}
        instat_bufof = {}
        ip_laststat = {}

        def g_ipw(g, oc, buf):
            w = oc_width.get(oc, 128)
            return g.dma_start(
                out=instat[:, buf, :, 0:w],
                in_=self.ipw_d[:, oc * 1024:(oc + 1) * 1024].rearrange(
                    "p (dc o) -> p dc o", dc=8)[:, :, 0:w])

        for i, oc in enumerate((24, 25)):
            emi("gpsimd", lambda g, _oc=oc, _b=i: g_ipw(g, _oc, _b), sW, 16)
            instat_ready[oc] = sW.v
            instat_bufof[oc] = i

        wait("tensor", sW, instat_ready[25])
        a_mm = [0] * 8
        for ft in range(8):
            pa = P[ft % 4]
            self.te_bank(pa)
            def t24(t, _ft=ft, _pa=pa):
                r = None
                for dc in range(8):
                    r = t.matmul(_pa[0:16, :], instat[:, instat_bufof[24], dc, 0:16],
                                 xn[:, dc, _ft * 512:(_ft + 1) * 512],
                                 start=(dc == 0), stop=(dc == 7))
                return r
            emi("tensor", t24, sTE)
            a_mm[ft] = sTE.v
            pg_ = P[4 + ft % 2]
            self.te_bank(pg_)
            def t25(t, _ft=ft, _pg=pg_):
                r = None
                for dc in range(8):
                    r = t.matmul(_pg[0:8, :], instat[:, instat_bufof[25], dc, 0:8],
                                 xn[:, dc, _ft * 512:(_ft + 1) * 512],
                                 start=(dc == 0), stop=(dc == 7))
                return r
            emi("tensor", t25, sTE)
            gate_mm = sTE.v
            if ft < 4:
                wait("vector", sTE, a_mm[ft])
                emi("vector", lambda v, _ft=ft, _pa=pa: v.tensor_scalar(
                    out=af1[:, (_ft % 4) * 512:(_ft % 4 + 1) * 512], in0=_pa[0:16, :],
                    scalar1=self.wbase[:, 0:1], scalar2=None, op0=AL.mult), sVE)
                self.consumed(pa, sVE)
            wait("vector", sTE, gate_mm)
            emi("vector", lambda v, _pg=pg_: v.tensor_copy(glog[:, :], _pg[0:8, :]), sVE)
            self.consumed(pg_, sVE)
            wait("tensor", sVE)
            self.te_bank(PG)
            def t_gtr(t):
                r = None
                for i in range(4):
                    r = t.transpose(PG[:, i * 8:(i + 1) * 8],
                                    glog[:, i * 128:(i + 1) * 128], idf32[0:8, 0:8])
                return r
            emi("tensor", t_gtr, sTE)
            wait("vector", sTE)
            def v_gev(v, _ft=ft):
                r = None
                for i in range(4):
                    r = v.tensor_copy(rta[:, 0, (_ft * 4 + i) * 8:(_ft * 4 + i + 1) * 8],
                                      PG[:, i * 8:(i + 1) * 8])
                return r
            emi("vector", v_gev, sVE)
            self.consumed(PG, sVE)
        ip_laststat[24] = sTE.v

        af1v = af1.ap().rearrange("p (c l) -> p c l", l=64)
        af2v = af2.ap().rearrange("p (c l) -> p c l", l=64)
        af1v3 = af1v
        af2v3 = af2v

        def bc(sl):
            return asm[:, sl].unsqueeze(2).broadcast_to([16, 32, 64])

        def chunk_scan(dst, src, op1, init):
            def v_s(v, _d=dst, _s=src, _o=op1, _i=init):
                r = None
                for c in range(32):
                    r = v.tensor_tensor_scan(_d[:, c * 64:(c + 1) * 64],
                                             zcol[0:16, 0:1].broadcast_to([16, 64]),
                                             _s[:, c * 64:(c + 1) * 64], _i, AL.add, _o)
                return r
            emi("vector", v_s, sVE)

        def tr_to_tm(b, src, tm_dst, from_se):
            if from_se:
                wait("tensor", sSE)
            else:
                wait("tensor", sVE)
            self.te_bank(PG)
            def t_tr(t, _src=src):
                r = None
                for i in range(16):
                    r = t.transpose(PG[:, i * 16:(i + 1) * 16],
                                    _src[:, i * 128:(i + 1) * 128], idf32[0:16, 0:16])
                return r
            emi("tensor", t_tr, sTE)
            wait("vector", sTE)
            if tm_dst == "rs":
                em("vector", lambda v, _b=b: v.tensor_copy(
                    norm_tm[:, _b * 256:(_b + 1) * 256], PG[:, 0:256]))
                emi("vector", lambda v, _b=b: v.tensor_tensor(
                    norm_tm[:, _b * 256:(_b + 1) * 256],
                    norm_tm[:, _b * 256:(_b + 1) * 256],
                    erm_tm[:, _b * 256:(_b + 1) * 256], AL.mult), sVE)
            elif tm_dst is not None:
                emi("vector", lambda v, _b=b, _d=tm_dst: v.tensor_copy(
                    _d[:, _b * 256:(_b + 1) * 256], PG[:, 0:256]), sVE)
            else:
                emi("vector", lambda v, _b=b: v.tensor_tensor(
                    norm_tm[:, _b * 256:(_b + 1) * 256],
                    norm_tm[:, _b * 256:(_b + 1) * 256], PG[:, 0:256], AL.add), sVE)
            self.consumed(PG, sVE)
            return sTE.v

        def a_pipe(b):
            chunk_scan(af2, af1, AL.add, 0.0)
            em("vector", lambda v: v.tensor_copy(asm[:, AS["tmp"]], af2v[:, :, 63]))
            self.vfill()
            em("vector", lambda v: v.tensor_tensor_scan(
                asm[:, AS["csa"]], zcol[0:16, 0:1].broadcast_to([16, 32]),
                asm[:, AS["tmp"]], 0.0, AL.add, AL.add))
            self.vfill()
            em("vector", lambda v: v.tensor_tensor_scan(
                asm[:, AS["tmp"]], zcol[0:16, 0:1].broadcast_to([16, 32]),
                asm[:, AS["csa"]], 3.0e38, AL.add, AL.min))
            self.vfill()
            em("vector", lambda v: v.tensor_scalar(
                out=asm[:, AS["tmp"]], in0=asm[:, AS["tmp"]], scalar1=0.0,
                scalar2=None, op0=AL.min))
            em("vector", lambda v: v.memset(
                asm[:, AS["rmp"].start:AS["rmp"].start + 1], 0.0))
            em("vector", lambda v: v.tensor_copy(
                asm[:, AS["rmp"].start + 1:AS["rmp"].stop],
                asm[:, AS["tmp"].start:AS["tmp"].stop - 1]))
            chunk_scan(af1, af2, AL.min, 3.0e38)
            em("vector", lambda v: v.tensor_copy(asm[:, AS["cmin"]], af1v[:, :, 63]))
            emi("vector", lambda v: v.engine_nop(), sVE)
            wait("scalar", sVE)
            emi("scalar", lambda s: s.activation(af1[:, :], af1[:, :], AF.Exp), sSE)
            tr_to_tm(b, af1, erm_tm, True)
            wait("scalar", sVE)
            wait("scalar", sTE)
            emi("scalar", lambda s: s.activation(af1[:, :], af2[:, :], AF.Exp,
                                                 scale=self.neg1[0:16, 0:1]), sSE)
            tr_to_tm(b, af1, ena_tm, True)
            wait("vector", sSE)
            wait("vector", sTE)
            chunk_scan(af1, af1, AL.add, 0.0)
            em("vector", lambda v: v.tensor_copy(asm[:, AS["rs63"]], af1v[:, :, 63]))
            emi("vector", lambda v: v.engine_nop(), sVE)
            tr_to_tm(b, af1, "rs", False)
            wait("vector", sTE)
            chunk_scan(af1, af2, AL.max, -3.0e38)
            em("vector", lambda v: v.tensor_copy(asm[:, AS["cmax"]], af1v[:, :, 63]))
            self.vfill()
            em("vector", lambda v: v.tensor_tensor(af1v3, af2v3,
                                                   bc(AS["cmax"]), AL.subtract))
            emi("vector", lambda v: v.tensor_tensor(af1v3, af1v3,
                                                    bc(AS["rmp"]), AL.add), sVE)
            wait("scalar", sVE)
            emi("scalar", lambda s: s.activation(af1[:, :], af1[:, :], AF.Exp), sSE)
            tr_to_tm(b, af1, sdo_tm, True)
            em("vector", lambda v: v.tensor_tensor(asm[:, AS["tmp"]], asm[:, AS["cmin"]],
                                                   asm[:, AS["csa"]], AL.subtract))
            emi("vector", lambda v: v.engine_nop(), sVE)
            wait("scalar", sVE)
            emi("scalar", lambda s: s.activation(asm[:, AS["tmp"]], asm[:, AS["tmp"]],
                                                 AF.Exp), sSE)
            wait("vector", sSE)
            em("vector", lambda v: v.tensor_tensor(asm[:, AS["ndp"]], asm[:, AS["tmp"]],
                                                   asm[:, AS["rs63"]], AL.mult))
            em("vector", lambda v: v.memset(
                asm[:, AS["ndc"].start:AS["ndc"].start + 1], 0.0))
            em("vector", lambda v: v.tensor_copy(
                asm[:, AS["ndc"].start + 1:AS["ndc"].stop],
                asm[:, AS["ndp"].start:AS["ndp"].stop - 1]))
            self.vfill()
            em("vector", lambda v: v.tensor_tensor_scan(
                asm[:, AS["ndc"]], zcol[0:16, 0:1].broadcast_to([16, 32]),
                asm[:, AS["ndc"]], 0.0, AL.add, AL.add))
            self.vfill()
            wait("vector", sTE)
            emi("vector", lambda v: v.tensor_tensor(af1v3, af1v3,
                                                    bc(AS["ndc"]), AL.mult), sVE)
            tr_to_tm(b, af1, None, False)
            wait("vector", sTE)
            em("vector", lambda v: v.memset(af1[:, :], 0.0))
            self.vfill()
            emi("vector", lambda v: v.tensor_tensor(af1v3, af1v3,
                                                    bc(AS["tmp"]), AL.add), sVE)
            tr_to_tm(b, af1, dsp_tm, False)

        a_pipe(0)
        wait("vector", sTE)
        for ft in range(4, 8):
            wait("vector", sTE, a_mm[ft])
            emi("vector", lambda v, _ft=ft: v.tensor_scalar(
                out=af1[:, (_ft % 4) * 512:(_ft % 4 + 1) * 512],
                in0=P[_ft % 4][0:16, :], scalar1=self.wbase[:, 0:1], scalar2=None,
                op0=AL.mult), sVE)
            self.consumed(P[ft % 4], sVE)
        a_pipe(1)
        A_TE_END = sTE.v
        emi("vector", lambda v: v.reciprocal(norm_tm[:, :], norm_tm[:, :]), sVE)
        em("vector", lambda v: v.tensor_tensor(ydsc[:, :], erm_tm[:, :],
                                               norm_tm[:, :], AL.mult))
        emi("vector", lambda v: v.tensor_tensor(yosc[:, :], sdo_tm[:, :],
                                                norm_tm[:, :], AL.mult), sVE)

        p3 = rta.ap().rearrange("p s (t e) -> p s t e", e=8)
        R_ALL_W = 0
        if stage >= 3:
            em("vector", lambda v: v.tensor_reduce(
                nse[:, :], p3[:, 0, :, :], AX, AL.max, negate=True))
            self.vfill()
            emi("vector", lambda v: v.tensor_tensor(
                p3[:, 1, :, :], p3[:, 0, :, :],
                nse.ap().unsqueeze(2).broadcast_to([128, 32, 8]), AL.add), sVE)
            wait("scalar", sVE)
            emi("scalar", lambda s: s.activation(rta[:, 1, :], rta[:, 1, :], AF.Exp), sSE)
            wait("vector", sSE)
            em("vector", lambda v: v.tensor_reduce(
                nse[:, :], p3[:, 1, :, :], AX, AL.add))
            self.vfill()
            em("vector", lambda v: v.reciprocal(nse[:, :], nse[:, :]))
            self.vfill()
            em("vector", lambda v: v.tensor_tensor(
                p3[:, 1, :, :], p3[:, 1, :, :],
                nse.ap().unsqueeze(2).broadcast_to([128, 32, 8]), AL.mult))
            em("vector", lambda v: v.memset(rta[:, 2, :], 0.0))
            for d in range(1, 8):
                em("vector", lambda v, _d=d: v.tensor_tensor(
                    p3[:, 0, :, 0:8 - _d], p3[:, 1, :, 0:8 - _d], p3[:, 1, :, _d:8],
                    AL.is_ge))
                em("vector", lambda v, _d=d: v.tensor_tensor(
                    p3[:, 2, :, _d:8], p3[:, 2, :, _d:8], p3[:, 0, :, 0:8 - _d], AL.add))
                em("vector", lambda v, _d=d: v.tensor_tensor(
                    p3[:, 0, :, 0:8 - _d], p3[:, 1, :, _d:8], p3[:, 1, :, 0:8 - _d],
                    AL.is_gt))
                em("vector", lambda v, _d=d: v.tensor_tensor(
                    p3[:, 2, :, 0:8 - _d], p3[:, 2, :, 0:8 - _d], p3[:, 0, :, 0:8 - _d],
                    AL.add))
            for tt in range(32):
                sl = slice(tt * 8, (tt + 1) * 8)
                em("vector", lambda v, _sl=sl: v.max(out=rta[:, 0, _sl],
                                                     in_=rta[:, 1, _sl]))
            for tt in range(32):
                em("vector", lambda v, _tt=tt: v.memset(
                    rta[:, 3, _tt * 8:_tt * 8 + 1], 0.0))
                em("vector", lambda v, _tt=tt: v.tensor_copy(
                    rta[:, 3, _tt * 8 + 1:_tt * 8 + 8], rta[:, 0, _tt * 8:_tt * 8 + 7]))
            for tt in range(32):
                em("vector", lambda v, _tt=tt: v.tensor_tensor_scan(
                    rta[:, 3, _tt * 8:_tt * 8 + 8], zcol[:, 0:1].broadcast_to([128, 8]),
                    rta[:, 3, _tt * 8:_tt * 8 + 8], 0.0, AL.add, AL.add))
            em("vector", lambda v: v.tensor_scalar(
                out=rta[:, 0, :], in0=rta[:, 3, :], scalar1=0.9, scalar2=None,
                op0=AL.is_lt))
            self.vfill()
            em("vector", lambda v: v.tensor_reduce(
                nse[:, :], p3[:, 0, :, :], AX, AL.add))
            self.vfill()
            em("vector", lambda v: v.tensor_tensor(
                p3[:, 4, :, :], p3[:, 2, :, :],
                nse.ap().unsqueeze(2).broadcast_to([128, 32, 8]), AL.is_lt))
            em("vector", lambda v: v.tensor_tensor(rta[:, 3, :], rta[:, 1, :],
                                                   rta[:, 2, :], AL.subtract))
            em("vector", lambda v: v.tensor_scalar(
                out=rta[:, 3, :], in0=rta[:, 3, :], scalar1=8.0, scalar2=None,
                op0=AL.add))
            em("vector", lambda v: v.tensor_tensor(rta[:, 3, :], rta[:, 3, :],
                                                   rta[:, 4, :], AL.mult))
            emi("vector", lambda v: v.tensor_scalar(
                out=rta[:, 3, :], in0=rta[:, 3, :], scalar1=9.0, scalar2=None,
                op0=AL.subtract), sVE)
            stg_free = [0]
            for g in range(8):
                wait("tensor", sVE)
                self.te_bank(PG)
                def t_rtr(t, _g=g):
                    r = None
                    for i in range(4):
                        tt = _g * 4 + i
                        r = t.transpose(PG[0:8, i * 128:(i + 1) * 128],
                                        rta[:, 3, tt * 8:(tt + 1) * 8], idf32[:, :])
                    return r
                emi("tensor", t_rtr, sTE)
                wait("vector", sTE)
                wait("vector", sW, stg_free[0])
                def v_rev(v, _g=g):
                    r = None
                    for i in range(4):
                        r = v.tensor_copy(stg8[:, i * 128:(i + 1) * 128],
                                          PG[0:8, i * 128:(i + 1) * 128])
                    return r
                emi("vector", v_rev, sVE)
                self.consumed(PG, sVE)
                wait("gpsimd", sVE)
                emi("gpsimd", lambda g_, _g=g: g_.dma_start(
                    out=self.cc_in[:, _g * 512:(_g + 1) * 512],
                    in_=stg8[:, :]), sW, 16)
                stg_free[0] = sW.v
            wait("gpsimd", sW)
            emi("gpsimd", lambda g: g.collective_compute(
                "AllGather", AL.bypass, replica_groups=[list(range(NCORES))],
                ins=[self.cc_in.ap()], outs=[self.cc_out.ap()]), sCC)
            wait("gpsimd", sCC)
            def g_rall(g):
                r = None
                cov = self.cc_out.ap().rearrange("(c e) t -> c e t", e=8)
                for e_ in range(8):
                    r = g.dma_start(out=r_all[e_ * 8:(e_ + 1) * 8, :],
                                    in_=cov[:, e_, :])
                    if e_ < 7:
                        r.then_inc(sW.sem, 16)
                return r
            emi("gpsimd", g_rall, sW, 16)
            sW.v += 16 * 7
            R_ALL_W = sW.v

        def ssd_hp(hp, b, diag_ve):
            for q in range(getattr(self, "q_n", 4)):
                fbase = b * 2048 + q * 512
                for ri, oc in enumerate([hp, 8 + hp, 16 + hp]):
                    pin = P[(q + ri) % 2]
                    wait("tensor", sW, instat_ready[oc])
                    self.te_bank(pin)
                    def t_ip(t, _oc=oc, _pin=pin, _fb=fbase):
                        r = None
                        for dc in range(8):
                            r = t.matmul(_pin[:, :], instat[:, instat_bufof[_oc], dc, :],
                                         xn[:, dc, _fb:_fb + 512],
                                         start=(dc == 0), stop=(dc == 7))
                        return r
                    emi("tensor", t_ip, sTE)
                    ipmm = sTE.v
                    self.ip_done_te = sTE.v
                    wait("scalar", sTE, ipmm)
                    g = self.bank_free.get(("strip", ri))
                    if g:
                        wait("scalar", g[0], g[1])
                    g2 = self.bank_free.get(("carry", ri))
                    if g2:
                        wait("scalar", g2[0], g2[1])
                    emi("scalar", lambda s, _ri=ri, _pin=pin: s.activation(
                        strip[:, _ri, 3:515], _pin[:, :], AF.Copy), sSE)
                    self.consumed(pin, sSE)
                    wait("vector", sSE)
                    if q == 0:
                        em("vector", lambda v, _ri=ri: v.memset(strip[:, _ri, 0:3], 0.0))
                    else:
                        em("vector", lambda v, _ri=ri: v.tensor_copy(
                            strip[:, _ri, 0:3], carry[:, _ri, 0:3]))
                    emi("vector", lambda v, _ri=ri: v.tensor_copy(
                        carry[:, _ri, 0:3], strip[:, _ri, 512:515]), sVE)
                    self.bank_free[("pads", ri)] = (sVE, sVE.v)
                    self.bank_free[("carry", ri)] = (sVE, sVE.v)
                if getattr(self, "ssd_lim", 9) < 2:
                    continue
                for ri in range(3):
                    pcv = P[2 + ri % 2]
                    self.te_bank(pcv)
                    wait("tensor", sSE)
                    wait("tensor", sVE, diag_ve)
                    g = self.bank_free[("pads", ri)]
                    wait("tensor", g[0], g[1])
                    def t_cv(t, _ri=ri, _p=pcv):
                        r = None
                        for k in range(4):
                            r = t.matmul(_p[:, :], diag[:, _ri, k, :],
                                         strip[:, _ri, k:k + 512],
                                         start=(k == 0), stop=(k == 3))
                        return r
                    emi("tensor", t_cv, sTE)
                    self.bank_free[("strip", ri)] = (sTE, sTE.v)
                    oc = [hp, 8 + hp, 16 + hp][ri]
                    wait("scalar", sTE)
                    if ri == 0:
                        dst = cfm[:, q * 512:(q + 1) * 512]
                    elif ri == 1:
                        dst = bfm[:, :]
                    else:
                        dst = shx[:, :]
                    emi("scalar", lambda s, _p=pcv, _oc=oc, _d=dst: s.activation(
                        _d, _p[:, :], AF.Identity, bias=self.convb[:, _oc:_oc + 1]), sSE)
                    self.consumed(pcv, sSE)
                    self.bank_free[("cv", ri)] = (sSE, sSE.v)
                if getattr(self, "ssd_lim", 9) < 3:
                    continue
                g1 = self.bank_free[("cv", 1)]
                wait("tensor", g1[0], g1[1])
                g2 = self.bank_free[("cv", 2)]
                wait("tensor", g2[0], g2[1])
                self.te_bank(PT)
                def t_bx(t):
                    r = None
                    for i in range(4):
                        r = t.transpose(PT[:, i * 128:(i + 1) * 128],
                                        bfm[:, i * 128:(i + 1) * 128], idbf[:, :])
                    for i in range(4):
                        r = t.transpose(PT[:, (4 + i) * 128:(5 + i) * 128],
                                        shx[:, i * 128:(i + 1) * 128], idbf[:, :])
                    return r
                emi("tensor", t_bx, sTE)
                self.bank_free[("bxsrc",)] = (sTE, sTE.v)
                wait("vector", sTE)
                def v_bx(v, _q=q, _b=b, _hp=hp):
                    r = None
                    for i in range(4):
                        cp = _q * 4 + i
                        for h2 in range(2):
                            col = _b * 256 + cp * 16 + _hp * 2 + h2
                            r = v.tensor_scalar(
                                out=btm[:, i, h2 * 64:(h2 + 1) * 64],
                                in0=PT[:, i * 128 + h2 * 64:i * 128 + (h2 + 1) * 64],
                                scalar1=dsp_tm[:, col:col + 1], scalar2=None,
                                op0=AL.mult)
                            r = v.tensor_scalar(
                                out=xtm[:, cp, h2 * 64:(h2 + 1) * 64],
                                in0=PT[:, (4 + i) * 128 + h2 * 64:
                                       (4 + i) * 128 + (h2 + 1) * 64],
                                scalar1=ena_tm[:, col:col + 1], scalar2=None,
                                op0=AL.mult)
                    return r
                emi("vector", v_bx, sVE)
                self.consumed(PT, sVE)
                if getattr(self, "ssd_lim", 9) < 4:
                    continue
                self.te_bank(P[2])
                self.te_bank(P[3])
                wait("tensor", sVE)
                def t_g(t, _q=q):
                    r = None
                    for j in range(4):
                        for h2 in range(2):
                            for par in range(2):
                                cl = 2 * j + par
                                c = _q * 8 + cl
                                r = t.matmul(
                                    P[2 + h2][par * 64:(par + 1) * 64,
                                              j * 64:(j + 1) * 64],
                                    bfm[h2 * 64:(h2 + 1) * 64, cl * 64:(cl + 1) * 64],
                                    cfm[h2 * 64:(h2 + 1) * 64, c * 64:(c + 1) * 64],
                                    start=True, stop=True,
                                    tile_position=(h2 * 64, par * 64))
                    return r
                emi("tensor", t_g, sTE)
                wait("vector", sTE)
                def v_mt(v, _q=q):
                    r = None
                    for h2 in range(2):
                        r = v.tensor_tensor(
                            mt[:, _q * 4:(_q + 1) * 4, h2, :],
                            P[2 + h2].ap().rearrange("p (j l) -> p j l", l=64)[:, 0:4, :],
                            self.trimask.ap().unsqueeze(1).broadcast_to([128, 4, 64]),
                            AL.mult)
                    return r
                emi("vector", v_mt, sVE)
                self.consumed(P[2], sVE)
                self.consumed(P[3], sVE)
                if getattr(self, "ssd_lim", 9) < 5:
                    continue
                self.te_bank(P[4])
                self.te_bank(P[5])
                def t_st(t, _q=q):
                    r = None
                    for j in range(4):
                        for par in range(2):
                            cl = 2 * j + par
                            for h2 in range(2):
                                r = t.matmul(
                                    P[4 + par][h2 * 64:(h2 + 1) * 64,
                                               j * 64:(j + 1) * 64],
                                    btm[par * 64:(par + 1) * 64, j,
                                        h2 * 64:(h2 + 1) * 64],
                                    xtm[par * 64:(par + 1) * 64, _q * 4 + j,
                                        h2 * 64:(h2 + 1) * 64],
                                    start=True, stop=True,
                                    tile_position=(par * 64, h2 * 64))
                    return r
                emi("tensor", t_st, sTE)
                wait("vector", sTE)
                def v_sp(v, _q=q):
                    r = None
                    for par in range(2):
                        r = v.tensor_copy(
                            spu[:, par, 1 + _q * 4:5 + _q * 4, :],
                            P[4 + par].ap().rearrange("p (j pp) -> p j pp", pp=64)[
                                :, 0:4, :])
                    return r
                emi("vector", v_sp, sVE)
                self.consumed(P[4], sVE)
                self.consumed(P[5], sVE)
                if q == 0:
                    em("vector", lambda v: v.memset(spu[:, 0, 0:1, :], 0.0))
                    em("vector", lambda v: v.memset(spu[:, 1, 0:1, :], 0.0))
            if getattr(self, "ssd_lim", 9) < 6:
                return
            self.vfill()
            for s in range(1, 17):
                for par in range(2):
                    em("vector", lambda v, _s=s, _p=par: v.tensor_tensor(
                        spu[:, _p, _s, :], spu[:, _p, _s, :],
                        spu[:, _p, _s - 1, :], AL.add))
            emi("vector", lambda v: v.nop(), sVE)
            scan_done = sVE.v
            if getattr(self, "ssd_lim", 9) < 7:
                return
            for g in range(4):
                wait("tensor", sVE, scan_done)
                for pk in (2, 3, 4, 5):
                    self.te_bank(P[pk])
                def t_y(t, _g=g):
                    r = None
                    for j in range(4):
                        for par in range(2):
                            c = _g * 8 + 2 * j + par
                            cp = _g * 4 + j
                            for h2 in range(2):
                                slot = j * 2 + h2
                                r = t.matmul(
                                    P[2 + par][par * 64:(par + 1) * 64,
                                               slot * 64:(slot + 1) * 64],
                                    mt[par * 64:(par + 1) * 64, cp, h2, :],
                                    xtm[par * 64:(par + 1) * 64, cp,
                                        h2 * 64:(h2 + 1) * 64],
                                    start=True, stop=True,
                                    tile_position=(par * 64, par * 64))
                                r = t.matmul(
                                    P[4 + h2][par * 64:(par + 1) * 64,
                                              j * 64:(j + 1) * 64],
                                    cfm[h2 * 64:(h2 + 1) * 64, c * 64:(c + 1) * 64],
                                    spu[h2 * 64:(h2 + 1) * 64, 0, c // 2 + par, :],
                                    start=True, stop=False,
                                    tile_position=(h2 * 64, par * 64))
                                r = t.matmul(
                                    P[4 + h2][par * 64:(par + 1) * 64,
                                              j * 64:(j + 1) * 64],
                                    cfm[h2 * 64:(h2 + 1) * 64, c * 64:(c + 1) * 64],
                                    spu[h2 * 64:(h2 + 1) * 64, 1, c // 2, :],
                                    start=False, stop=True,
                                    tile_position=(h2 * 64, par * 64))
                    return r
                emi("tensor", t_y, sTE)
                wait("vector", sTE)
                def v_y(v, _g=g, _b=b, _hp=hp):
                    for h2 in range(2):
                        yob = yosc.ap().rearrange(
                            "p (bb cp h) -> p bb cp h", bb=2, cp=16)[
                            :, _b, _g * 4:(_g + 1) * 4,
                            _hp * 2 + h2].unsqueeze(2).broadcast_to([128, 4, 64])
                        pv = P[4 + h2].ap().rearrange("p (j l) -> p j l", l=64)[
                            :, 0:4, :]
                        v.tensor_tensor(pv, pv, yob, AL.mult)
                    for par in range(2):
                        pr = slice(par * 64, (par + 1) * 64)
                        ydb = ydsc.ap().rearrange(
                            "p (bb cp h) -> p bb cp h", bb=2, cp=16)[
                            pr, _b, _g * 4:(_g + 1) * 4,
                            _hp * 2:_hp * 2 + 2].unsqueeze(3).broadcast_to(
                            [64, 4, 2, 64])
                        v.tensor_tensor(
                            yts[pr, _g * 4:(_g + 1) * 4, :, :],
                            P[2 + par].ap().rearrange("p (j a l) -> p j a l",
                                                      a=2, l=64)[pr, :, :, :],
                            ydb, AL.mult)
                    r = None
                    for h2 in range(2):
                        r = v.tensor_tensor(
                            yts[:, _g * 4:(_g + 1) * 4, h2, :],
                            yts[:, _g * 4:(_g + 1) * 4, h2, :],
                            P[4 + h2].ap().rearrange("p (j l) -> p j l", l=64)[
                                :, 0:4, :],
                            AL.add)
                    return r
                emi("vector", v_y, sVE)
                for pk in (2, 3, 4, 5):
                    self.consumed(P[pk], sVE)
            if getattr(self, "ssd_lim", 9) < 8:
                return
            for half in range(2):
                self.te_bank(PT)
                wait("tensor", sVE)
                def t_yt(t, _h=half):
                    r = None
                    for i in range(8):
                        cp = _h * 8 + i
                        r = t.transpose(PT[:, i * 128:(i + 1) * 128],
                                        yts.ap().rearrange("p cp a l -> p cp (a l)")[
                                            :, cp, :], idbf[:, :])
                    return r
                emi("tensor", t_yt, sTE)
                wait("vector", sTE)
                def v_yt(v, _h=half, _b=b, _hp=hp):
                    r = None
                    for i in range(8):
                        cp = _h * 8 + i
                        o = _b * 2048 + cp * 128
                        r = v.tensor_copy(ya[:, _hp, o:o + 128],
                                          PT[:, i * 128:(i + 1) * 128])
                    return r
                emi("vector", v_yt, sVE)
                self.consumed(PT, sVE)

        if stage >= 4:
            for hp in range(getattr(self, "hp_n", 8)):
                for ri, oc in enumerate([hp, 8 + hp, 16 + hp]):
                    wait("gpsimd", sTE, ip_laststat.get(ri, ip_laststat.get(24, 0)))
                    emi("gpsimd", lambda g, _oc=oc, _b=ri: g_ipw(g, _oc, _b), sW, 16)
                    instat_ready[oc] = sW.v
                    instat_bufof[oc] = ri
                wait("vector", sTE)
                def v_diag(v, _hp=hp):
                    r = None
                    for ri, oc in enumerate([_hp, 8 + _hp, 16 + _hp]):
                        for k in range(4):
                            r = v.tensor_scalar(
                                out=diag[:, ri, k, :], in0=idbf[:, :],
                                scalar1=self.convw[:, oc * 4 + k:oc * 4 + k + 1],
                                scalar2=None, op0=AL.mult)
                    return r
                emi("vector", v_diag, sVE)
                diag_ve = sVE.v
                for b in range(B):
                    ssd_hp(hp, b, diag_ve)
                for ri in range(3):
                    ip_laststat[ri] = self.ip_done_te
        HP_TE = sTE.v
        HP_VE = sVE.v
        hpctx.close()
        tailctx = ExitStack()
        srch = self.sb("srch", [64, 512], F32,
                       actx if (self.dumps and stage < 5) else tailctx)

        if stage >= 3:
            wait("vector", sW, R_ALL_W)
            wait("vector", sTE, HP_TE)
            em("vector", lambda v: v.memset(scol[:, 0:1], -9.0))
            em("vector", lambda v: v.memset(scol[:, 1:2], 0.0))
            masks = [[((i % 8) ^ s) + (i // 8) * 8 for i in range(32)] for s in (4, 2, 1)]
            HV = [slice(0, 32), slice(32, 64)]
            def VF():
                emi("vector", lambda v: v.engine_nop(), sVE)
                wait("vector", sVE)
            for it in range(SEARCH_ITERS):
                for hv in HV:
                    em("vector", lambda v, _h=hv: v.tensor_scalar(
                        out=scol[_h, 2:3], in0=scol[_h, 0:1], scalar1=scol[_h, 1:2],
                        scalar2=0.5, op0=AL.add, op1=AL.mult))
                VF()
                for seg in range(8):
                    for hv in HV:
                        em("vector", lambda v, _h=hv, _s=seg: v.tensor_scalar(
                            out=srch[_h, :], in0=r_all[_h, _s * 512:(_s + 1) * 512],
                            scalar1=scol[_h, 2:3], scalar2=None, op0=AL.is_gt,
                            op1=AL.add, accum_out=scnt[_h, _s:_s + 1]))
                VF()
                for hv in HV:
                    em("vector", lambda v, _h=hv: v.tensor_tensor(
                        scnt[_h, 0:4], scnt[_h, 0:4], scnt[_h, 4:8], AL.add))
                VF()
                for hv in HV:
                    em("vector", lambda v, _h=hv: v.tensor_tensor(
                        scnt[_h, 0:2], scnt[_h, 0:2], scnt[_h, 2:4], AL.add))
                VF()
                for hv in HV:
                    em("vector", lambda v, _h=hv: v.tensor_tensor(
                        scol[_h, 3:4], scnt[_h, 0:1], scnt[_h, 1:2], AL.add))
                VF()
                for m in masks:
                    for hv in HV:
                        em("vector", lambda v, _h=hv, _m=m: v.stream_shuffle(
                            scol[_h, 6:7], scol[_h, 3:4], _m))
                    VF()
                    for hv in HV:
                        em("vector", lambda v, _h=hv: v.tensor_tensor(
                            scol[_h, 3:4], scol[_h, 3:4], scol[_h, 6:7], AL.add))
                    VF()
                for hv in HV:
                    em("vector", lambda v, _h=hv: v.tensor_scalar(
                        out=scol[_h, 5:6], in0=scol[_h, 3:4], scalar1=float(CAP),
                        scalar2=None, op0=AL.is_ge))
                VF()
                for hv in HV:
                    em("vector", lambda v, _h=hv: v.tensor_tensor(
                        scol[_h, 6:7], scol[_h, 2:3], scol[_h, 0:1], AL.subtract))
                VF()
                for hv in HV:
                    em("vector", lambda v, _h=hv: v.tensor_scalar(
                        out=scol[_h, 6:7], in0=scol[_h, 6:7], scalar1=scol[_h, 5:6],
                        scalar2=None, op0=AL.mult))
                VF()
                for hv in HV:
                    em("vector", lambda v, _h=hv: v.tensor_tensor(
                        scol[_h, 0:1], scol[_h, 0:1], scol[_h, 6:7], AL.add))
                for hv in HV:
                    em("vector", lambda v, _h=hv: v.tensor_tensor(
                        scol[_h, 7:8], scol[_h, 1:2], scol[_h, 2:3], AL.subtract))
                VF()
                for hv in HV:
                    em("vector", lambda v, _h=hv: v.tensor_scalar(
                        out=scol[_h, 7:8], in0=scol[_h, 7:8], scalar1=scol[_h, 5:6],
                        scalar2=None, op0=AL.mult))
                VF()
                for hv in HV:
                    em("vector", lambda v, _h=hv: v.tensor_tensor(
                        scol[_h, 1:2], scol[_h, 2:3], scol[_h, 7:8], AL.add))
                VF()
            emi("vector", lambda v: v.engine_nop(), sVE)
            wait("tensor", sVE)
            self.te_bank(PG)
            emi("tensor", lambda t: t.transpose(PG[0:1, 0:64], scol[:, 0:1],
                                                idf32[0:64, 0:64]), sTE)
            wait("vector", sTE)
            emi("vector", lambda v: v.tensor_copy(
                lo8[:, :], PG.ap()[0:1, 0:64].rearrange("p (e c) -> p e c", c=8)[
                    :, :, 0]), sVE)
            self.consumed(PG, sVE)
            wait("tensor", sVE)
            self.te_bank(PG)
            emi("tensor", lambda t: t.matmul(PG[:, 0:8], self.ones1[:, :], lo8[:, :],
                                             start=True, stop=True), sTE)
            wait("vector", sTE)
            emi("vector", lambda v: v.tensor_copy(thr8[:, :], PG[:, 0:8]), sVE)
            self.consumed(PG, sVE)
            self.vfill()
            self.vfill()
            em("vector", lambda v: v.tensor_tensor(
                p3[:, 0, :, :], p3[:, 3, :, :],
                thr8.ap().unsqueeze(1).broadcast_to([128, 32, 8]), AL.is_gt))
            em("vector", lambda v: v.tensor_tensor(rta[:, 0, :], rta[:, 0, :],
                                                   rta[:, 4, :], AL.mult))
            emi("vector", lambda v: v.tensor_tensor(
                self.sc_tm.ap().rearrange("p a b -> p (a b)"), rta[:, 0, :],
                rta[:, 1, :], AL.mult), sVE)
        SCORES_VE = sVE.v
        self.SCORES_VE = SCORES_VE

        if stage >= 5:
            opstat = self.sb("opstat", [128, 8, 8, 128], BF16, tailctx)
            wait("gpsimd", sTE, HP_TE)
            wait("gpsimd", sVE, HP_VE)
            emi("gpsimd", lambda g: g.dma_start(
                out=opstat[:, :, :, :],
                in_=self.opw_d.ap().rearrange("p (oc dc o) -> p oc dc o",
                                              oc=8, dc=8)), sW, 16)
            OP_W = sW.v
            wait("tensor", sW, OP_W)
            for ft in range(16):
                for oc in range(8):
                    pb = P[oc // 2]
                    if oc % 2 == 0:
                        self.te_bank(pb)
                    def t_op(t, _oc=oc, _ft=ft, _pb=pb):
                        r = None
                        for kc in range(8):
                            r = t.matmul(
                                _pb[:, (_oc % 2) * 256:(_oc % 2 + 1) * 256],
                                opstat[:, _oc, kc, :],
                                ya[:, kc, _ft * 256:(_ft + 1) * 256],
                                start=(kc == 0), stop=(kc == 7))
                        return r
                    emi("tensor", t_op, sTE)
                wait("vector", sTE)
                def v_op(v, _ft=ft):
                    r = None
                    for oc in range(8):
                        r = v.tensor_copy(
                            ya[:, oc, _ft * 256:(_ft + 1) * 256],
                            P[oc // 2][:, (oc % 2) * 256:(oc % 2 + 1) * 256])
                    return r
                emi("vector", v_op, sVE)
                for j in range(4):
                    self.consumed(P[j], sVE)
        self.run_block("attn")
        tailctx.close()
        PERS = {"xn", "ya", "sc_tm", "scol", "scnt", "lo8", "thr8"}
        needs_attn = any(t in self.attn_handles and t not in PERS
                         for t in self.dumps)
        if self.stage >= 6 and not needs_attn:
            actx.close()
            self.actx_open = None
        else:
            self.actx_open = actx

    def phase_ffn(self):
        nc = self.nc
        self.new_sched()
        em, emi, wait = self.em, self.emi, self.wait
        sX, sW, sTE, sVE, sSE, sOUT = (self.sX, self.sW, self.sTE, self.sVE,
                                       self.sSE, self.sOUT)
        P, PT, PG = self.P, self.PT, self.PG
        xn, ya = self.xn, self.ya
        self.bank_free = {}

        fctx = ExitStack()
        onesr = self.sb("onesr", [8, 128], BF16, fctx)
        wb1 = self.sb("wb1", [128, 8, 8, 128], BF16, fctx)
        wb2 = self.sb("wb2", [128, 8, 8, 128], BF16, fctx)
        hbuf = self.sb("hbuf", [128, 8, 512], BF16, fctx)
        sb_s = self.sb("sb_s", [128, 512], F32, fctx)
        sstg = self.sb("sstg", [8, 512], BF16, fctx)
        xrl = self.sb("xrl", [128, 2, 1024], F32, fctx)
        outst = self.sb("outst", [128, 2, 1024], F32, fctx)

        emi("gpsimd", lambda g: g.dma_start(
            out=wb1[:, :, :, :],
            in_=self.w1_d[0].rearrange("p (dc fc o) -> p dc fc o",
                                       dc=8, fc=8)), sW, 16)
        w1_rdy = sW.v
        emi("gpsimd", lambda g: g.dma_start(
            out=wb2[:, :, :, :],
            in_=self.w2_d[0].rearrange("p (fc oc o) -> p fc oc o",
                                       fc=8, oc=8)), sW, 16)
        w2_rdy = sW.v

        w2_mm_end = 0
        sbcast_last = 0
        for e in range(8):
            wait("gpsimd", sTE, sbcast_last)
            def g_oh(g, _e=e):
                g.memset(onesr[:, :], 1.0)
                return g.affine_select(out=onesr[:, :], in_=onesr[:, :],
                                       compare_op=AL.is_equal, fill=0.0, base=-_e,
                                       pattern=[[0, 128]], channel_multiplier=1)
            emi("gpsimd", g_oh, sW)
            onesr_rdy = sW.v
            wait("tensor", sW, w1_rdy)
            w1_done_at = None
            for ft in range(8):
                fb = ft * 512
                self.te_bank(PG)
                def t_str(t, _ft=ft):
                    r = None
                    for i in range(4):
                        tt = _ft * 4 + i
                        r = t.transpose(PG[0:8, i * 128:(i + 1) * 128],
                                        self.sc_tm.ap().rearrange("p a b -> p (a b)")[
                                            :, tt * 8:(tt + 1) * 8], self.idf32[:, :])
                    return r
                emi("tensor", t_str, sTE)
                wait("vector", sTE)
                def v_sev(v):
                    r = None
                    for i in range(4):
                        r = v.tensor_copy(sstg[:, i * 128:(i + 1) * 128],
                                          PG[0:8, i * 128:(i + 1) * 128])
                    return r
                emi("vector", v_sev, sVE)
                self.consumed(PG, sVE)
                wait("tensor", sVE)
                wait("tensor", sW, onesr_rdy)
                self.te_bank(PG)
                emi("tensor", lambda t: t.matmul(
                    PG[:, :], onesr[:, :], sstg[:, :],
                    start=True, stop=True), sTE)
                sbcast_last = sTE.v
                wait("vector", sTE)
                emi("vector", lambda v: v.tensor_copy(sb_s[:, :], PG[:, :]), sVE)
                self.consumed(PG, sVE)
                for fc in range(8):
                    ph = P[fc % 2]
                    self.te_bank(ph)
                    def t_w1(t, _fc=fc, _fb=fb, _ph=ph):
                        r = None
                        for dc in range(8):
                            r = t.matmul(_ph[:, :], wb1[:, dc, _fc, :],
                                         xn[:, dc, _fb:_fb + 512],
                                         start=(dc == 0), stop=(dc == 7))
                        return r
                    emi("tensor", t_w1, sTE)
                    wait("scalar", sTE)
                    wait("scalar", sTE, w2_mm_end)
                    emi("scalar", lambda s, _fc=fc, _ph=ph: s.activation(
                        hbuf[:, _fc, :], _ph[:, :], AF.Gelu), sSE)
                    self.consumed(ph, sSE)
                if ft == 7:
                    w1_done_at = sTE.v
                wait("vector", sSE)
                emi("vector", lambda v: v.tensor_tensor(
                    hbuf[:, :, :],
                    hbuf[:, :, :],
                    sb_s.ap().unsqueeze(1).broadcast_to([128, 8, 512]),
                    AL.mult), sVE)
                hprime = sVE.v
                wait("tensor", sW, w2_rdy)
                wait("tensor", sVE, hprime)
                for oc in range(8):
                    py = P[2 + oc % 2]
                    self.te_bank(py)
                    def t_w2(t, _oc=oc, _fb=fb, _py=py):
                        r = None
                        for fc in range(8):
                            r = t.matmul(_py[:, :], wb2[:, fc, _oc, :],
                                         hbuf[:, fc, :],
                                         start=(fc == 0), stop=(fc == 7))
                        return r
                    emi("tensor", t_w2, sTE)
                    wait("vector", sTE)
                    emi("vector", lambda v, _oc=oc, _fb=fb, _py=py: v.tensor_tensor(
                        ya[:, _oc, _fb:_fb + 512], ya[:, _oc, _fb:_fb + 512],
                        _py[:, :], AL.add), sVE)
                    self.consumed(py, sVE)
                w2_mm_end = sTE.v
            w2_done_at = sTE.v
            if e + 1 < 8:
                wait("gpsimd", sTE, w1_done_at)
                emi("gpsimd", lambda g, _e=e: g.dma_start(
                    out=wb1[:, :, :, :],
                    in_=self.w1_d[_e + 1].rearrange(
                        "p (dc fc o) -> p dc fc o", dc=8, fc=8)), sW, 16)
                w1_rdy = sW.v
                wait("gpsimd", sTE, w2_done_at)
                emi("gpsimd", lambda g, _e=e: g.dma_start(
                    out=wb2[:, :, :, :],
                    in_=self.w2_d[_e + 1].rearrange(
                        "p (fc oc o) -> p fc oc o", fc=8, oc=8)), sW, 16)
                w2_rdy = sW.v

        wait("tensor", sVE)
        xrl_free = [0, 0]
        out_free = [0, 0]
        for tt in range(32):
            pp = tt % 2
            wait("sync", sVE, xrl_free[pp])
            emi("sync", lambda s, _pp=pp, _tt=tt: s.dma_start(
                out=xrl[:, _pp, :], in_=self.x_d[_tt * 128:(_tt + 1) * 128, :]), sX, 16)
            self.te_bank(PT)
            def t_ep(t, _tt=tt):
                r = None
                for dc in range(8):
                    r = t.transpose(PT[:, dc * 128:(dc + 1) * 128],
                                    ya[:, dc, _tt * 128:(_tt + 1) * 128],
                                    self.idbf[:, :])
                return r
            emi("tensor", t_ep, sTE)
            wait("vector", sTE)
            wait("vector", sX)
            wait("vector", sOUT, out_free[pp])
            emi("vector", lambda v, _pp=pp: v.tensor_tensor(
                outst[:, _pp, :], PT[:, :], xrl[:, _pp, :], AL.add), sVE)
            self.consumed(PT, sVE)
            xrl_free[pp] = sVE.v
            wait("sync", sVE)
            emi("sync", lambda s, _pp=pp, _tt=tt: s.dma_start(
                out=self.out_d[_tt * 128:(_tt + 1) * 128, :], in_=outst[:, _pp, :]),
                sOUT, 16)
            out_free[pp] = sOUT.v
        wait("sync", sOUT, 16 * 32)
        self.run_block("ffn")
        fctx.close()

    def phase_dumps(self):
        if not self.dumps and self.stage >= 6:
            return
        nc = self.nc
        self.new_sched()
        if self.stage < 6:
            self.emi("gpsimd", lambda g: g.dma_start(
                out=self.out_d.ap(), in_=self.x_d.ap()), self.sOUT, 16)
        for tag in self.dumps:
            h = self.attn_handles.get(tag) if hasattr(self, "attn_handles") else None
            srct = {"xn": self.xn, "ya": self.ya, "sc_tm": self.sc_tm}.get(tag, h)
            if srct is None:
                continue
            ap = srct[tuple([slice(None)] * len(srct.shape))] if hasattr(srct, "shape") \
                else srct
            dd = nc.dram_tensor(f"dbg_{tag}", list(ap.shape), ap.dtype,
                                kind="ExternalOutput")
            self.emi("gpsimd", lambda g, _d=dd, _a=ap: g.dma_start(
                out=_d.ap(), in_=_a), self.sOUT, 16)
        self.wait("gpsimd", self.sOUT)
        self.run_block("dumps")


def build_program(stage=STAGE, dumps=(), hp_n=8, q_n=4):
    nc = bass.Bass("TRN2", target_bir_lowering=False)
    b = Builder(nc, stage, dumps)
    b.hp_n = hp_n
    b.q_n = q_n
    return b.build()


def build_program_lim(stage, dumps, hp_n, q_n, ssd_lim):
    nc = bass.Bass("TRN2", target_bir_lowering=False)
    b = Builder(nc, stage, dumps)
    b.hp_n = hp_n
    b.q_n = q_n
    b.ssd_lim = ssd_lim
    return b.build()


def _prep_weights(in_proj_w, gate_w, conv_w, conv_b, w_base, out_proj_w, w1, w2):
    f32 = np.float32
    Wa = np.zeros((26 * 128, 1024), dtype=f32)
    Wa[:3072] = in_proj_w[:3072]
    Wa[24 * 128:24 * 128 + 16] = in_proj_w[3072:3088]
    Wa[25 * 128:25 * 128 + 8] = gate_w
    ipw = np.ascontiguousarray(
        Wa.reshape(26, 128, 8, 128).transpose(3, 0, 2, 1).reshape(128, 26 * 1024))
    convw = np.ascontiguousarray(
        conv_w.reshape(3072, 4).reshape(24, 128, 4).transpose(1, 0, 2).reshape(128, 96))
    convb = np.ascontiguousarray(conv_b.reshape(24, 128).T)
    wbase = np.ascontiguousarray(w_base.reshape(16, 1))
    opw = np.ascontiguousarray(
        out_proj_w.reshape(8, 128, 8, 128).transpose(3, 0, 2, 1).reshape(128, 8192))
    w1l = np.ascontiguousarray(
        w1.reshape(8, 8, 128, 8, 128).transpose(0, 4, 3, 1, 2).reshape(8, 128, 8192))
    w2l = np.ascontiguousarray(
        w2.reshape(8, 8, 128, 8, 128).transpose(0, 4, 3, 1, 2).reshape(8, 128, 8192))
    return ipw, convw, convb, wbase, opw, w1l, w2l


def kernel(L0_attn_x, L0_attn_in_proj_w, L0_attn_conv_w, L0_attn_conv_b,
           L0_attn_w_base, L0_attn_out_proj_w, L0_ffn_gate_w, L0_ffn_w1, L0_ffn_w2):
    x = np.asarray(L0_attn_x, dtype=np.float32)
    ipw, convw, convb, wbase, opw, w1l, w2l = _prep_weights(
        np.asarray(L0_attn_in_proj_w, dtype=np.float32),
        np.asarray(L0_ffn_gate_w, dtype=np.float32),
        np.asarray(L0_attn_conv_w, dtype=np.float32),
        np.asarray(L0_attn_conv_b, dtype=np.float32),
        np.asarray(L0_attn_w_base, dtype=np.float32),
        np.asarray(L0_attn_out_proj_w, dtype=np.float32),
        np.asarray(L0_ffn_w1, dtype=np.float32),
        np.asarray(L0_ffn_w2, dtype=np.float32))

    nc = build_program(stage=6, dumps=())
    in_maps = []
    for c in range(NCORES):
        xs = np.ascontiguousarray(x[2 * c:2 * c + 2].reshape(NTOK, D))
        in_maps.append(dict(x=xs, ipw=ipw, convw=convw, convb=convb, wbase=wbase,
                            opw=opw, w1=w1l, w2=w2l))
    res = run_bass_kernel_spmd(nc, in_maps, core_ids=list(range(NCORES)))
    out = np.concatenate([r["out"].reshape(2, T, D) for r in res.results], axis=0)
    return out


# version 03
# speedup vs baseline: 1.0249x; 1.0249x over previous
import numpy as np
import concourse.bass as bass
import concourse.mybir as mybir
from concourse.bass_utils import run_bass_kernel_spmd
from contextlib import ExitStack

F32 = mybir.dt.float32
BF16 = mybir.dt.bfloat16
AL = mybir.AluOpType
AF = mybir.ActivationFunctionType
AX = mybir.AxisListType.X

B = 2
T = 2048
NTOK = B * T
DC = 8
HPAIRS = 8
CAP = 4096
NCORES = 8
SEARCH_ITERS = 32
D = 1024

STAGE = 6


class Builder:
    def __init__(self, nc, stage, dumps):
        self.nc = nc
        self.stage = stage
        self.dumps = dumps
        self.ctx = ExitStack()
        self.sched = None
        self.bank_free = {}
        self.dump_aps = []

    def sb(self, name, shape, dt, ctx=None):
        return (ctx or self.ctx).enter_context(self.nc.sbuf_tensor(name, shape, dt))

    def psum(self, name, shape, dt):
        return self.ctx.enter_context(self.nc.psum_tensor(name, shape, dt))

    def sem(self, name):
        class Ct:
            __slots__ = ("sem", "v")
        c = Ct()
        c.sem = self.ctx.enter_context(self.nc.semaphore(name))
        c.v = 0
        return c

    def new_sched(self):
        self.sched = {k: [] for k in ("tensor", "vector", "scalar", "sync", "gpsimd")}

    def run_block(self, name):
        s = self.sched
        with self.nc.Block(name) as blk:
            for eng in ("gpsimd", "sync", "scalar", "vector", "tensor"):
                if s[eng]:
                    getattr(blk, eng)(
                        (lambda ops: lambda e: [f(e) for f in ops])(tuple(s[eng])))

    def em(self, eng, f):
        self.sched[eng].append(f)

    def emi(self, eng, f, ct, n=1):
        ct.v += n
        self.sched[eng].append(lambda e, _f=f, _s=ct.sem, _n=n: _f(e).then_inc(_s, _n))

    def wait(self, eng, ct, v=None):
        vv = ct.v if v is None else v
        if vv <= 0:
            return
        self.sched[eng].append(lambda e, _s=ct.sem, _v=vv: e.wait_ge(_s, _v))

    def vfill(self):
        self.sched["vector"].append(
            lambda v: v.tensor_copy(self.vjunk[:, 0:32], self.vjunk[:, 32:64]))

    def te_bank(self, pobj):
        g = self.bank_free.get(id(pobj))
        if g is not None:
            self.wait("tensor", g[0], g[1])

    def consumed(self, pobj, ct):
        self.bank_free[id(pobj)] = (ct, ct.v)

    def build(self):
        nc = self.nc
        stage = self.stage

        self.x_d = nc.dram_tensor("x", [NTOK, D], F32, kind="ExternalInput")
        self.ipw_d = nc.dram_tensor("ipw", [128, 26 * 1024], F32, kind="ExternalInput")
        self.convw_d = nc.dram_tensor("convw", [128, 96], F32, kind="ExternalInput")
        self.convb_d = nc.dram_tensor("convb", [128, 24], F32, kind="ExternalInput")
        self.wbase_d = nc.dram_tensor("wbase", [16, 1], F32, kind="ExternalInput")
        self.opw_d = nc.dram_tensor("opw", [128, 8192], F32, kind="ExternalInput")
        self.w1_d = nc.dram_tensor("w1", [8, 128, 8192], F32, kind="ExternalInput")
        self.w2_d = nc.dram_tensor("w2", [8, 128, 8192], F32, kind="ExternalInput")
        self.out_d = nc.dram_tensor("out", [NTOK, D], F32, kind="ExternalOutput")
        self.cc_in = nc.dram_tensor("cc_in", [8, NTOK], F32)
        self.cc_out = nc.dram_tensor("cc_out", [NCORES * 8, NTOK], F32)

        self.xn = self.sb("xn", [128, DC, NTOK], BF16)
        self.ya = self.sb("ya", [128, DC, NTOK], BF16)
        self.sc_tm = self.sb("sc_tm", [128, 32, 8], F32)
        self.idbf = self.sb("idbf", [128, 128], BF16)
        self.idf32 = self.sb("idf32", [128, 128], F32)
        self.trimask = self.sb("trimask", [128, 64], BF16)
        self.ones1 = self.sb("ones1", [1, 128], F32)
        self.wbase = self.sb("wbase_s", [16, 1], F32)
        self.convw = self.sb("convw_s", [128, 96], F32)
        self.convb = self.sb("convb_s", [128, 24], F32)
        self.zcol = self.sb("zcol", [128, 1], F32)
        self.epsc = self.sb("epsc", [128, 1], F32)
        self.neg1 = self.sb("neg1", [128, 1], F32)
        self.vjunk = self.sb("vjunk", [128, 64], BF16)

        self.P = [self.psum(f"P{i}", [128, 512], F32) for i in range(6)]
        self.PT = self.psum("PT", [128, 1024], BF16)
        self.PG = self.psum("PG", [128, 512], F32)

        self.sX = self.sem("sX")
        self.sW = self.sem("sW")
        self.sTE = self.sem("sTE")
        self.sVE = self.sem("sVE")
        self.sSE = self.sem("sSE")
        self.sCC = self.sem("sCC")
        self.sOUT = self.sem("sOUT")

        self.phase_ln()
        if stage >= 2:
            self.phase_attn()
        if stage >= 6:
            self.phase_ffn()
        self.phase_dumps()
        if getattr(self, "actx_open", None) is not None:
            self.actx_open.close()
        self.ctx.close()
        return nc

    def phase_ln(self):
        nc = self.nc
        self.new_sched()
        em, emi, wait = self.em, self.emi, self.wait
        sX, sW, sTE, sVE, sSE = self.sX, self.sW, self.sTE, self.sVE, self.sSE
        idbf, idf32, trimask, ones1, zcol = (
            self.idbf, self.idf32, self.trimask, self.ones1, self.zcol)
        PT = self.PT
        xn = self.xn

        def g_consts(g):
            g.memset(idbf[:, :], 0.0)
            g.affine_select(out=idbf[:, :], in_=idbf[:, :], compare_op=AL.not_equal,
                            fill=1.0, base=0, pattern=[[-1, 128]], channel_multiplier=1)
            g.memset(idf32[:, :], 0.0)
            g.affine_select(out=idf32[:, :], in_=idf32[:, :], compare_op=AL.not_equal,
                            fill=1.0, base=0, pattern=[[-1, 128]], channel_multiplier=1)
            g.memset(trimask[:, :], 1.0)
            g.affine_select(out=trimask[0:64, :], in_=trimask[0:64, :],
                            compare_op=AL.is_ge, fill=0.0, base=0,
                            pattern=[[1, 64]], channel_multiplier=-1)
            g.affine_select(out=trimask[64:128, :], in_=trimask[64:128, :],
                            compare_op=AL.is_ge, fill=0.0, base=0,
                            pattern=[[1, 64]], channel_multiplier=-1)
            g.memset(ones1[:, :], 1.0)
            g.memset(zcol[:, :], 0.0)
            g.memset(self.epsc[:, :], 1e-5)
            g.memset(self.neg1[:, :], -1.0)
            g.dma_start(out=self.wbase[:, :], in_=self.wbase_d.ap()).then_inc(
                sW.sem, 16)
            g.dma_start(out=self.convw[:, :], in_=self.convw_d.ap()).then_inc(
                sW.sem, 16)
            return g.dma_start(out=self.convb[:, :], in_=self.convb_d.ap())
        emi("gpsimd", g_consts, sW, 16)
        sW.v += 32
        self.W_SMALL = sW.v

        lnctx = ExitStack()
        bn2all = self.sb("bn2all", [128, 32, 8], F32,
                         self.ctx if "bn2" in self.dumps else lnctx)
        xt = self.sb("xt", [128, 2, 1024], F32, lnctx)
        xsq = self.sb("xsq", [128, 1024], F32, lnctx)
        xn_tm = self.sb("xn_tm", [128, 2, 1024], BF16, lnctx)
        if not hasattr(self, "attn_handles"):
            self.attn_handles = {}
        self.attn_handles["bn2"] = bn2all

        wait("tensor", sW, self.W_SMALL)
        wait("scalar", sW, self.W_SMALL)
        wait("vector", sW, self.W_SMALL)
        xt_free = [0, 0]
        pt_free = 0
        for tt in range(32):
            pp = tt % 2
            b2 = bn2all[:, tt, :]
            wait("sync", sVE, xt_free[pp])
            emi("sync", lambda s, _pp=pp, _tt=tt: s.dma_start(
                out=xt[:, _pp, :], in_=self.x_d[_tt * 128:(_tt + 1) * 128, :]), sX, 16)
            wait("vector", sX, 16 * (tt + 1))
            wait("vector", sTE, pt_free)
            em("vector", lambda v, _pp=pp, _b=b2: v.tensor_scalar(
                out=xn_tm[:, _pp, :], in0=xt[:, _pp, :], scalar1=1.0 / 1024,
                scalar2=None, op0=AL.mult, op1=AL.add, accum_out=_b[:, 0:1]))
            em("vector", lambda v, _pp=pp: v.tensor_tensor(
                xsq[:, :], xt[:, _pp, :], xt[:, _pp, :], AL.mult))
            em("vector", lambda v, _pp=pp, _b=b2: v.tensor_scalar(
                out=xn_tm[:, _pp, :], in0=xsq[:, :], scalar1=1.0 / 1024,
                scalar2=None, op0=AL.mult, op1=AL.add, accum_out=_b[:, 1:2]))
            em("vector", lambda v, _b=b2: v.tensor_tensor(_b[:, 4:5], _b[:, 0:1],
                                                          _b[:, 0:1], AL.mult))
            self.vfill()
            emi("vector", lambda v, _b=b2: v.tensor_tensor(_b[:, 1:2], _b[:, 1:2],
                                                           _b[:, 4:5], AL.subtract), sVE)
            wait("scalar", sVE)
            emi("scalar", lambda s, _b=b2: s.activation(_b[:, 2:3], _b[:, 1:2], AF.Sqrt,
                                                        bias=self.epsc[:, 0:1]), sSE)
            wait("vector", sSE)
            em("vector", lambda v, _b=b2: v.reciprocal(_b[:, 3:4], _b[:, 2:3]))
            self.vfill()
            emi("vector", lambda v, _pp=pp, _b=b2: v.tensor_scalar(
                out=xn_tm[:, _pp, :], in0=xt[:, _pp, :], scalar1=_b[:, 0:1],
                scalar2=_b[:, 3:4], op0=AL.subtract, op1=AL.mult), sVE)
            xt_free[pp] = sVE.v
            wait("tensor", sVE)
            self.te_bank(PT)
            def t_tr(t, _pp=pp):
                r = None
                for dc in range(8):
                    r = t.transpose(PT[:, dc * 128:(dc + 1) * 128],
                                    xn_tm[:, _pp, dc * 128:(dc + 1) * 128], idbf[:, :])
                return r
            emi("tensor", t_tr, sTE)
            pt_free = sTE.v
            wait("vector", sTE)
            def v_ev(v, _tt=tt):
                r = None
                for dc in range(8):
                    r = v.tensor_copy(xn[:, dc, _tt * 128:(_tt + 1) * 128],
                                      PT[:, dc * 128:(dc + 1) * 128])
                return r
            emi("vector", v_ev, sVE)
            self.consumed(PT, sVE)

        self.run_block("ln")
        lnctx.close()

    def phase_attn(self):
        nc = self.nc
        stage = self.stage
        self.new_sched()
        em, emi, wait = self.em, self.emi, self.wait
        sX, sW, sTE, sVE, sSE, sCC = (self.sX, self.sW, self.sTE, self.sVE,
                                      self.sSE, self.sCC)
        P, PT, PG = self.P, self.PT, self.PG
        xn, ya = self.xn, self.ya
        idbf, idf32 = self.idbf, self.idf32
        zcol = self.zcol

        scol = self.sb("scol", [64, 8], F32)
        scnt = self.sb("scnt", [64, 8], F32)
        lo8 = self.sb("lo8", [1, 8], F32)
        thr8 = self.sb("thr8", [128, 8], F32)
        actx = ExitStack()
        self.actx = actx
        ena_tm = self.sb("ena_tm", [128, 512], F32, actx)
        erm_tm = self.sb("erm_tm", [128, 512], F32, actx)
        sdo_tm = self.sb("sdo_tm", [128, 512], F32, actx)
        dsp_tm = self.sb("dsp_tm", [128, 512], F32, actx)
        norm_tm = self.sb("norm_tm", [128, 512], F32, actx)
        ydsc = self.sb("ydsc", [128, 512], F32, actx)
        yosc = self.sb("yosc", [128, 512], F32, actx)
        rta = self.sb("rta", [128, 5, 256], F32, actx)
        nse = self.sb("nse", [128, 32], F32, actx)
        stg8 = self.sb("stg8", [8, 512], F32, actx)
        r_all = self.sb("r_all", [64, 4096], F32, actx)
        auxctx = ExitStack()
        af1 = self.sb("af1", [16, 2048], F32, auxctx)
        af2 = self.sb("af2", [16, 2048], F32, auxctx)
        asm = self.sb("asm", [16, 256], F32, auxctx)
        auxctx.close()
        glog = stg8
        self.attn_handles = dict(ena_tm=ena_tm, erm_tm=erm_tm, sdo_tm=sdo_tm,
                                 dsp_tm=dsp_tm, norm_tm=norm_tm,
                                 ydsc=ydsc, yosc=yosc, rta=rta, r_all=r_all,
                                 scol=scol, thr8=thr8, scnt=scnt, lo8=lo8,
                                 af1=af1, af2=af2, asm=asm, nse=nse)

        hpctx = ExitStack()
        spu = self.sb("spu", [128, 2, 17, 64], BF16, hpctx)
        yts = self.sb("yts", [128, 16, 2, 64], BF16, hpctx)
        mt = self.sb("mt", [128, 16, 2, 64], BF16, hpctx)
        xtm = self.sb("xtm", [128, 16, 128], BF16, hpctx)
        instat = self.sb("instat", [128, 3, 8, 128], BF16, hpctx)
        strip = self.sb("strip", [128, 3, 524], BF16, hpctx)
        carry = self.sb("carry", [128, 3, 4], BF16, hpctx)
        diag = self.sb("diag", [128, 3, 4, 128], BF16, hpctx)
        cfm = self.sb("cfm", [128, 2048], BF16, hpctx)
        bfm = self.sb("bfm", [128, 512], BF16, hpctx)
        shx = self.sb("shx", [128, 512], BF16, hpctx)
        btm = self.sb("btm", [128, 4, 128], BF16, hpctx)
        self.attn_handles.update(cfm=cfm, xtm=xtm, spu=spu, mt=mt,
                                 yts=yts, btm=btm, bfm=bfm, shx=shx, strip=strip)

        AS = {k: slice(i * 32, (i + 1) * 32) for i, k in enumerate(
            ("csa", "rmp", "cmin", "cmax", "rs63", "ndp", "ndc", "tmp"))}

        oc_width = {24: 16, 25: 8}
        instat_ready = {}
        instat_bufof = {}
        ip_laststat = {}

        def g_ipw(g, oc, buf):
            w = oc_width.get(oc, 128)
            return g.dma_start(
                out=instat[:, buf, :, 0:w],
                in_=self.ipw_d[:, oc * 1024:(oc + 1) * 1024].rearrange(
                    "p (dc o) -> p dc o", dc=8)[:, :, 0:w])

        for i, oc in enumerate((24, 25)):
            emi("gpsimd", lambda g, _oc=oc, _b=i: g_ipw(g, _oc, _b), sW, 16)
            instat_ready[oc] = sW.v
            instat_bufof[oc] = i

        wait("tensor", sW, instat_ready[25])
        a_mm = [0] * 8
        for ft in range(8):
            pa = P[ft % 4]
            self.te_bank(pa)
            def t24(t, _ft=ft, _pa=pa):
                r = None
                for dc in range(8):
                    r = t.matmul(_pa[0:16, :], instat[:, instat_bufof[24], dc, 0:16],
                                 xn[:, dc, _ft * 512:(_ft + 1) * 512],
                                 start=(dc == 0), stop=(dc == 7))
                return r
            emi("tensor", t24, sTE)
            a_mm[ft] = sTE.v
            pg_ = P[4 + ft % 2]
            self.te_bank(pg_)
            def t25(t, _ft=ft, _pg=pg_):
                r = None
                for dc in range(8):
                    r = t.matmul(_pg[0:8, :], instat[:, instat_bufof[25], dc, 0:8],
                                 xn[:, dc, _ft * 512:(_ft + 1) * 512],
                                 start=(dc == 0), stop=(dc == 7))
                return r
            emi("tensor", t25, sTE)
            gate_mm = sTE.v
            if ft < 4:
                wait("vector", sTE, a_mm[ft])
                emi("vector", lambda v, _ft=ft, _pa=pa: v.tensor_scalar(
                    out=af1[:, (_ft % 4) * 512:(_ft % 4 + 1) * 512], in0=_pa[0:16, :],
                    scalar1=self.wbase[:, 0:1], scalar2=None, op0=AL.mult), sVE)
                self.consumed(pa, sVE)
            wait("vector", sTE, gate_mm)
            emi("vector", lambda v, _pg=pg_: v.tensor_copy(glog[:, :], _pg[0:8, :]), sVE)
            self.consumed(pg_, sVE)
            wait("tensor", sVE)
            self.te_bank(PG)
            def t_gtr(t):
                r = None
                for i in range(4):
                    r = t.transpose(PG[:, i * 8:(i + 1) * 8],
                                    glog[:, i * 128:(i + 1) * 128], idf32[0:8, 0:8])
                return r
            emi("tensor", t_gtr, sTE)
            wait("vector", sTE)
            def v_gev(v, _ft=ft):
                r = None
                for i in range(4):
                    r = v.tensor_copy(rta[:, 0, (_ft * 4 + i) * 8:(_ft * 4 + i + 1) * 8],
                                      PG[:, i * 8:(i + 1) * 8])
                return r
            emi("vector", v_gev, sVE)
            self.consumed(PG, sVE)
        ip_laststat[24] = sTE.v

        af1v = af1.ap().rearrange("p (c l) -> p c l", l=64)
        af2v = af2.ap().rearrange("p (c l) -> p c l", l=64)
        af1v3 = af1v
        af2v3 = af2v

        def bc(sl):
            return asm[:, sl].unsqueeze(2).broadcast_to([16, 32, 64])

        def chunk_scan(dst, src, op1, init):
            def v_s(v, _d=dst, _s=src, _o=op1, _i=init):
                r = None
                for c in range(32):
                    r = v.tensor_tensor_scan(_d[:, c * 64:(c + 1) * 64],
                                             zcol[0:16, 0:1].broadcast_to([16, 64]),
                                             _s[:, c * 64:(c + 1) * 64], _i, AL.add, _o)
                return r
            emi("vector", v_s, sVE)

        def tr_to_tm(b, src, tm_dst, from_se):
            if from_se:
                wait("tensor", sSE)
            else:
                wait("tensor", sVE)
            self.te_bank(PG)
            def t_tr(t, _src=src):
                r = None
                for i in range(16):
                    r = t.transpose(PG[:, i * 16:(i + 1) * 16],
                                    _src[:, i * 128:(i + 1) * 128], idf32[0:16, 0:16])
                return r
            emi("tensor", t_tr, sTE)
            wait("vector", sTE)
            if tm_dst == "rs":
                em("vector", lambda v, _b=b: v.tensor_copy(
                    norm_tm[:, _b * 256:(_b + 1) * 256], PG[:, 0:256]))
                emi("vector", lambda v, _b=b: v.tensor_tensor(
                    norm_tm[:, _b * 256:(_b + 1) * 256],
                    norm_tm[:, _b * 256:(_b + 1) * 256],
                    erm_tm[:, _b * 256:(_b + 1) * 256], AL.mult), sVE)
            elif tm_dst is not None:
                emi("vector", lambda v, _b=b, _d=tm_dst: v.tensor_copy(
                    _d[:, _b * 256:(_b + 1) * 256], PG[:, 0:256]), sVE)
            else:
                emi("vector", lambda v, _b=b: v.tensor_tensor(
                    norm_tm[:, _b * 256:(_b + 1) * 256],
                    norm_tm[:, _b * 256:(_b + 1) * 256], PG[:, 0:256], AL.add), sVE)
            self.consumed(PG, sVE)
            return sTE.v

        def a_pipe(b):
            chunk_scan(af2, af1, AL.add, 0.0)
            em("vector", lambda v: v.tensor_copy(asm[:, AS["tmp"]], af2v[:, :, 63]))
            self.vfill()
            em("vector", lambda v: v.tensor_tensor_scan(
                asm[:, AS["csa"]], zcol[0:16, 0:1].broadcast_to([16, 32]),
                asm[:, AS["tmp"]], 0.0, AL.add, AL.add))
            self.vfill()
            em("vector", lambda v: v.tensor_tensor_scan(
                asm[:, AS["tmp"]], zcol[0:16, 0:1].broadcast_to([16, 32]),
                asm[:, AS["csa"]], 3.0e38, AL.add, AL.min))
            self.vfill()
            em("vector", lambda v: v.tensor_scalar(
                out=asm[:, AS["tmp"]], in0=asm[:, AS["tmp"]], scalar1=0.0,
                scalar2=None, op0=AL.min))
            em("vector", lambda v: v.memset(
                asm[:, AS["rmp"].start:AS["rmp"].start + 1], 0.0))
            em("vector", lambda v: v.tensor_copy(
                asm[:, AS["rmp"].start + 1:AS["rmp"].stop],
                asm[:, AS["tmp"].start:AS["tmp"].stop - 1]))
            chunk_scan(af1, af2, AL.min, 3.0e38)
            em("vector", lambda v: v.tensor_copy(asm[:, AS["cmin"]], af1v[:, :, 63]))
            emi("vector", lambda v: v.engine_nop(), sVE)
            wait("scalar", sVE)
            emi("scalar", lambda s: s.activation(af1[:, :], af1[:, :], AF.Exp), sSE)
            tr_to_tm(b, af1, erm_tm, True)
            wait("scalar", sVE)
            wait("scalar", sTE)
            emi("scalar", lambda s: s.activation(af1[:, :], af2[:, :], AF.Exp,
                                                 scale=self.neg1[0:16, 0:1]), sSE)
            tr_to_tm(b, af1, ena_tm, True)
            wait("vector", sSE)
            wait("vector", sTE)
            chunk_scan(af1, af1, AL.add, 0.0)
            em("vector", lambda v: v.tensor_copy(asm[:, AS["rs63"]], af1v[:, :, 63]))
            emi("vector", lambda v: v.engine_nop(), sVE)
            tr_to_tm(b, af1, "rs", False)
            wait("vector", sTE)
            chunk_scan(af1, af2, AL.max, -3.0e38)
            em("vector", lambda v: v.tensor_copy(asm[:, AS["cmax"]], af1v[:, :, 63]))
            self.vfill()
            em("vector", lambda v: v.tensor_tensor(af1v3, af2v3,
                                                   bc(AS["cmax"]), AL.subtract))
            emi("vector", lambda v: v.tensor_tensor(af1v3, af1v3,
                                                    bc(AS["rmp"]), AL.add), sVE)
            wait("scalar", sVE)
            emi("scalar", lambda s: s.activation(af1[:, :], af1[:, :], AF.Exp), sSE)
            tr_to_tm(b, af1, sdo_tm, True)
            em("vector", lambda v: v.tensor_tensor(asm[:, AS["tmp"]], asm[:, AS["cmin"]],
                                                   asm[:, AS["csa"]], AL.subtract))
            emi("vector", lambda v: v.engine_nop(), sVE)
            wait("scalar", sVE)
            emi("scalar", lambda s: s.activation(asm[:, AS["tmp"]], asm[:, AS["tmp"]],
                                                 AF.Exp), sSE)
            wait("vector", sSE)
            em("vector", lambda v: v.tensor_tensor(asm[:, AS["ndp"]], asm[:, AS["tmp"]],
                                                   asm[:, AS["rs63"]], AL.mult))
            em("vector", lambda v: v.memset(
                asm[:, AS["ndc"].start:AS["ndc"].start + 1], 0.0))
            em("vector", lambda v: v.tensor_copy(
                asm[:, AS["ndc"].start + 1:AS["ndc"].stop],
                asm[:, AS["ndp"].start:AS["ndp"].stop - 1]))
            self.vfill()
            em("vector", lambda v: v.tensor_tensor_scan(
                asm[:, AS["ndc"]], zcol[0:16, 0:1].broadcast_to([16, 32]),
                asm[:, AS["ndc"]], 0.0, AL.add, AL.add))
            self.vfill()
            wait("vector", sTE)
            emi("vector", lambda v: v.tensor_tensor(af1v3, af1v3,
                                                    bc(AS["ndc"]), AL.mult), sVE)
            tr_to_tm(b, af1, None, False)
            wait("vector", sTE)
            em("vector", lambda v: v.memset(af1[:, :], 0.0))
            self.vfill()
            emi("vector", lambda v: v.tensor_tensor(af1v3, af1v3,
                                                    bc(AS["tmp"]), AL.add), sVE)
            tr_to_tm(b, af1, dsp_tm, False)

        a_pipe(0)
        wait("vector", sTE)
        for ft in range(4, 8):
            wait("vector", sTE, a_mm[ft])
            emi("vector", lambda v, _ft=ft: v.tensor_scalar(
                out=af1[:, (_ft % 4) * 512:(_ft % 4 + 1) * 512],
                in0=P[_ft % 4][0:16, :], scalar1=self.wbase[:, 0:1], scalar2=None,
                op0=AL.mult), sVE)
            self.consumed(P[ft % 4], sVE)
        a_pipe(1)
        A_TE_END = sTE.v
        emi("vector", lambda v: v.reciprocal(norm_tm[:, :], norm_tm[:, :]), sVE)
        em("vector", lambda v: v.tensor_tensor(ydsc[:, :], erm_tm[:, :],
                                               norm_tm[:, :], AL.mult))
        emi("vector", lambda v: v.tensor_tensor(yosc[:, :], sdo_tm[:, :],
                                                norm_tm[:, :], AL.mult), sVE)

        p3 = rta.ap().rearrange("p s (t e) -> p s t e", e=8)
        R_ALL_W = 0
        if stage >= 3:
            em("vector", lambda v: v.tensor_reduce(
                nse[:, :], p3[:, 0, :, :], AX, AL.max, negate=True))
            self.vfill()
            emi("vector", lambda v: v.tensor_tensor(
                p3[:, 1, :, :], p3[:, 0, :, :],
                nse.ap().unsqueeze(2).broadcast_to([128, 32, 8]), AL.add), sVE)
            wait("scalar", sVE)
            emi("scalar", lambda s: s.activation(rta[:, 1, :], rta[:, 1, :], AF.Exp), sSE)
            wait("vector", sSE)
            em("vector", lambda v: v.tensor_reduce(
                nse[:, :], p3[:, 1, :, :], AX, AL.add))
            self.vfill()
            em("vector", lambda v: v.reciprocal(nse[:, :], nse[:, :]))
            self.vfill()
            em("vector", lambda v: v.tensor_tensor(
                p3[:, 1, :, :], p3[:, 1, :, :],
                nse.ap().unsqueeze(2).broadcast_to([128, 32, 8]), AL.mult))
            em("vector", lambda v: v.memset(rta[:, 2, :], 0.0))
            for d in range(1, 8):
                em("vector", lambda v, _d=d: v.tensor_tensor(
                    p3[:, 0, :, 0:8 - _d], p3[:, 1, :, 0:8 - _d], p3[:, 1, :, _d:8],
                    AL.is_ge))
                em("vector", lambda v, _d=d: v.tensor_tensor(
                    p3[:, 2, :, _d:8], p3[:, 2, :, _d:8], p3[:, 0, :, 0:8 - _d], AL.add))
                em("vector", lambda v, _d=d: v.tensor_tensor(
                    p3[:, 0, :, 0:8 - _d], p3[:, 1, :, _d:8], p3[:, 1, :, 0:8 - _d],
                    AL.is_gt))
                em("vector", lambda v, _d=d: v.tensor_tensor(
                    p3[:, 2, :, 0:8 - _d], p3[:, 2, :, 0:8 - _d], p3[:, 0, :, 0:8 - _d],
                    AL.add))
            for tt in range(32):
                sl = slice(tt * 8, (tt + 1) * 8)
                em("vector", lambda v, _sl=sl: v.max(out=rta[:, 0, _sl],
                                                     in_=rta[:, 1, _sl]))
            for tt in range(32):
                em("vector", lambda v, _tt=tt: v.memset(
                    rta[:, 3, _tt * 8:_tt * 8 + 1], 0.0))
                em("vector", lambda v, _tt=tt: v.tensor_copy(
                    rta[:, 3, _tt * 8 + 1:_tt * 8 + 8], rta[:, 0, _tt * 8:_tt * 8 + 7]))
            for tt in range(32):
                em("vector", lambda v, _tt=tt: v.tensor_tensor_scan(
                    rta[:, 3, _tt * 8:_tt * 8 + 8], zcol[:, 0:1].broadcast_to([128, 8]),
                    rta[:, 3, _tt * 8:_tt * 8 + 8], 0.0, AL.add, AL.add))
            em("vector", lambda v: v.tensor_scalar(
                out=rta[:, 0, :], in0=rta[:, 3, :], scalar1=0.9, scalar2=None,
                op0=AL.is_lt))
            self.vfill()
            em("vector", lambda v: v.tensor_reduce(
                nse[:, :], p3[:, 0, :, :], AX, AL.add))
            self.vfill()
            em("vector", lambda v: v.tensor_tensor(
                p3[:, 4, :, :], p3[:, 2, :, :],
                nse.ap().unsqueeze(2).broadcast_to([128, 32, 8]), AL.is_lt))
            em("vector", lambda v: v.tensor_tensor(rta[:, 3, :], rta[:, 1, :],
                                                   rta[:, 2, :], AL.subtract))
            em("vector", lambda v: v.tensor_scalar(
                out=rta[:, 3, :], in0=rta[:, 3, :], scalar1=8.0, scalar2=None,
                op0=AL.add))
            em("vector", lambda v: v.tensor_tensor(rta[:, 3, :], rta[:, 3, :],
                                                   rta[:, 4, :], AL.mult))
            emi("vector", lambda v: v.tensor_scalar(
                out=rta[:, 3, :], in0=rta[:, 3, :], scalar1=9.0, scalar2=None,
                op0=AL.subtract), sVE)
            stg_free = [0]
            for g in range(8):
                wait("tensor", sVE)
                self.te_bank(PG)
                def t_rtr(t, _g=g):
                    r = None
                    for i in range(4):
                        tt = _g * 4 + i
                        r = t.transpose(PG[0:8, i * 128:(i + 1) * 128],
                                        rta[:, 3, tt * 8:(tt + 1) * 8], idf32[:, :])
                    return r
                emi("tensor", t_rtr, sTE)
                wait("vector", sTE)
                wait("vector", sW, stg_free[0])
                def v_rev(v, _g=g):
                    r = None
                    for i in range(4):
                        r = v.tensor_copy(stg8[:, i * 128:(i + 1) * 128],
                                          PG[0:8, i * 128:(i + 1) * 128])
                    return r
                emi("vector", v_rev, sVE)
                self.consumed(PG, sVE)
                wait("gpsimd", sVE)
                emi("gpsimd", lambda g_, _g=g: g_.dma_start(
                    out=self.cc_in[:, _g * 512:(_g + 1) * 512],
                    in_=stg8[:, :]), sW, 16)
                stg_free[0] = sW.v
            wait("gpsimd", sW)
            emi("gpsimd", lambda g: g.collective_compute(
                "AllGather", AL.bypass, replica_groups=[list(range(NCORES))],
                ins=[self.cc_in.ap()], outs=[self.cc_out.ap()]), sCC)
            wait("gpsimd", sCC)
            def g_rall(g):
                r = None
                cov = self.cc_out.ap().rearrange("(c e) t -> c e t", e=8)
                for e_ in range(8):
                    r = g.dma_start(out=r_all[e_ * 8:(e_ + 1) * 8, :],
                                    in_=cov[:, e_, :])
                    if e_ < 7:
                        r.then_inc(sW.sem, 16)
                return r
            emi("gpsimd", g_rall, sW, 16)
            sW.v += 16 * 7
            R_ALL_W = sW.v

        def ssd_hp(hp, b, diag_ve):
            for q in range(getattr(self, "q_n", 4)):
                fbase = b * 2048 + q * 512
                for ri, oc in enumerate([hp, 8 + hp, 16 + hp]):
                    pin = P[(q + ri) % 2]
                    wait("tensor", sW, instat_ready[oc])
                    self.te_bank(pin)
                    def t_ip(t, _oc=oc, _pin=pin, _fb=fbase):
                        r = None
                        for dc in range(8):
                            r = t.matmul(_pin[:, :], instat[:, instat_bufof[_oc], dc, :],
                                         xn[:, dc, _fb:_fb + 512],
                                         start=(dc == 0), stop=(dc == 7))
                        return r
                    emi("tensor", t_ip, sTE)
                    ipmm = sTE.v
                    self.ip_done_te = sTE.v
                    wait("scalar", sTE, ipmm)
                    g = self.bank_free.get(("strip", ri))
                    if g:
                        wait("scalar", g[0], g[1])
                    g2 = self.bank_free.get(("carry", ri))
                    if g2:
                        wait("scalar", g2[0], g2[1])
                    emi("scalar", lambda s, _ri=ri, _pin=pin: s.activation(
                        strip[:, _ri, 3:515], _pin[:, :], AF.Copy), sSE)
                    self.consumed(pin, sSE)
                    wait("vector", sSE)
                    if q == 0:
                        em("vector", lambda v, _ri=ri: v.memset(strip[:, _ri, 0:3], 0.0))
                    else:
                        em("vector", lambda v, _ri=ri: v.tensor_copy(
                            strip[:, _ri, 0:3], carry[:, _ri, 0:3]))
                    emi("vector", lambda v, _ri=ri: v.tensor_copy(
                        carry[:, _ri, 0:3], strip[:, _ri, 512:515]), sVE)
                    self.bank_free[("pads", ri)] = (sVE, sVE.v)
                    self.bank_free[("carry", ri)] = (sVE, sVE.v)
                if getattr(self, "ssd_lim", 9) < 2:
                    continue
                for ri in range(3):
                    pcv = P[2 + ri % 2]
                    self.te_bank(pcv)
                    wait("tensor", sSE)
                    wait("tensor", sVE, diag_ve)
                    g = self.bank_free[("pads", ri)]
                    wait("tensor", g[0], g[1])
                    def t_cv(t, _ri=ri, _p=pcv):
                        r = None
                        for k in range(4):
                            r = t.matmul(_p[:, :], diag[:, _ri, k, :],
                                         strip[:, _ri, k:k + 512],
                                         start=(k == 0), stop=(k == 3))
                        return r
                    emi("tensor", t_cv, sTE)
                    self.bank_free[("strip", ri)] = (sTE, sTE.v)
                    oc = [hp, 8 + hp, 16 + hp][ri]
                    wait("scalar", sTE)
                    if ri == 0:
                        dst = cfm[:, q * 512:(q + 1) * 512]
                    elif ri == 1:
                        dst = bfm[:, :]
                    else:
                        dst = shx[:, :]
                    emi("scalar", lambda s, _p=pcv, _oc=oc, _d=dst: s.activation(
                        _d, _p[:, :], AF.Identity, bias=self.convb[:, _oc:_oc + 1]), sSE)
                    self.consumed(pcv, sSE)
                    self.bank_free[("cv", ri)] = (sSE, sSE.v)
                if getattr(self, "ssd_lim", 9) < 3:
                    continue
                g1 = self.bank_free[("cv", 1)]
                wait("tensor", g1[0], g1[1])
                g2 = self.bank_free[("cv", 2)]
                wait("tensor", g2[0], g2[1])
                self.te_bank(PT)
                def t_bx(t):
                    r = None
                    for i in range(4):
                        r = t.transpose(PT[:, i * 128:(i + 1) * 128],
                                        bfm[:, i * 128:(i + 1) * 128], idbf[:, :])
                    for i in range(4):
                        r = t.transpose(PT[:, (4 + i) * 128:(5 + i) * 128],
                                        shx[:, i * 128:(i + 1) * 128], idbf[:, :])
                    return r
                emi("tensor", t_bx, sTE)
                self.bank_free[("bxsrc",)] = (sTE, sTE.v)
                wait("vector", sTE)
                def v_bx(v, _q=q, _b=b, _hp=hp):
                    r = None
                    for i in range(4):
                        cp = _q * 4 + i
                        for h2 in range(2):
                            col = _b * 256 + cp * 16 + _hp * 2 + h2
                            r = v.tensor_scalar(
                                out=btm[:, i, h2 * 64:(h2 + 1) * 64],
                                in0=PT[:, i * 128 + h2 * 64:i * 128 + (h2 + 1) * 64],
                                scalar1=dsp_tm[:, col:col + 1], scalar2=None,
                                op0=AL.mult)
                            r = v.tensor_scalar(
                                out=xtm[:, cp, h2 * 64:(h2 + 1) * 64],
                                in0=PT[:, (4 + i) * 128 + h2 * 64:
                                       (4 + i) * 128 + (h2 + 1) * 64],
                                scalar1=ena_tm[:, col:col + 1], scalar2=None,
                                op0=AL.mult)
                    return r
                emi("vector", v_bx, sVE)
                self.consumed(PT, sVE)
                if getattr(self, "ssd_lim", 9) < 4:
                    continue
                self.te_bank(P[2])
                self.te_bank(P[3])
                wait("tensor", sVE)
                def t_g(t, _q=q):
                    r = None
                    for j in range(4):
                        for h2 in range(2):
                            for par in range(2):
                                cl = 2 * j + par
                                c = _q * 8 + cl
                                r = t.matmul(
                                    P[2 + h2][par * 64:(par + 1) * 64,
                                              j * 64:(j + 1) * 64],
                                    bfm[h2 * 64:(h2 + 1) * 64, cl * 64:(cl + 1) * 64],
                                    cfm[h2 * 64:(h2 + 1) * 64, c * 64:(c + 1) * 64],
                                    start=True, stop=True,
                                    tile_position=(h2 * 64, par * 64))
                    return r
                emi("tensor", t_g, sTE)
                wait("vector", sTE)
                def v_mt(v, _q=q):
                    r = None
                    for h2 in range(2):
                        r = v.tensor_tensor(
                            mt[:, _q * 4:(_q + 1) * 4, h2, :],
                            P[2 + h2].ap().rearrange("p (j l) -> p j l", l=64)[:, 0:4, :],
                            self.trimask.ap().unsqueeze(1).broadcast_to([128, 4, 64]),
                            AL.mult)
                    return r
                emi("vector", v_mt, sVE)
                self.consumed(P[2], sVE)
                self.consumed(P[3], sVE)
                if getattr(self, "ssd_lim", 9) < 5:
                    continue
                self.te_bank(P[4])
                self.te_bank(P[5])
                def t_st(t, _q=q):
                    r = None
                    for j in range(4):
                        for par in range(2):
                            cl = 2 * j + par
                            for h2 in range(2):
                                r = t.matmul(
                                    P[4 + par][h2 * 64:(h2 + 1) * 64,
                                               j * 64:(j + 1) * 64],
                                    btm[par * 64:(par + 1) * 64, j,
                                        h2 * 64:(h2 + 1) * 64],
                                    xtm[par * 64:(par + 1) * 64, _q * 4 + j,
                                        h2 * 64:(h2 + 1) * 64],
                                    start=True, stop=True,
                                    tile_position=(par * 64, h2 * 64))
                    return r
                emi("tensor", t_st, sTE)
                wait("vector", sTE)
                def v_sp(v, _q=q):
                    r = None
                    for par in range(2):
                        r = v.tensor_copy(
                            spu[:, par, 1 + _q * 4:5 + _q * 4, :],
                            P[4 + par].ap().rearrange("p (j pp) -> p j pp", pp=64)[
                                :, 0:4, :])
                    return r
                emi("vector", v_sp, sVE)
                self.consumed(P[4], sVE)
                self.consumed(P[5], sVE)
                if q == 0:
                    em("vector", lambda v: v.memset(spu[:, 0, 0:1, :], 0.0))
                    em("vector", lambda v: v.memset(spu[:, 1, 0:1, :], 0.0))
            if getattr(self, "ssd_lim", 9) < 6:
                return
            self.vfill()
            for s in range(1, 17):
                for par in range(2):
                    em("vector", lambda v, _s=s, _p=par: v.tensor_tensor(
                        spu[:, _p, _s, :], spu[:, _p, _s, :],
                        spu[:, _p, _s - 1, :], AL.add))
            emi("vector", lambda v: v.nop(), sVE)
            scan_done = sVE.v
            if getattr(self, "ssd_lim", 9) < 7:
                return
            for g in range(4):
                wait("tensor", sVE, scan_done)
                for pk in (2, 3, 4, 5):
                    self.te_bank(P[pk])
                def t_y(t, _g=g):
                    r = None
                    for j in range(4):
                        for par in range(2):
                            c = _g * 8 + 2 * j + par
                            cp = _g * 4 + j
                            for h2 in range(2):
                                slot = j * 2 + h2
                                r = t.matmul(
                                    P[2 + par][par * 64:(par + 1) * 64,
                                               slot * 64:(slot + 1) * 64],
                                    mt[par * 64:(par + 1) * 64, cp, h2, :],
                                    xtm[par * 64:(par + 1) * 64, cp,
                                        h2 * 64:(h2 + 1) * 64],
                                    start=True, stop=True,
                                    tile_position=(par * 64, par * 64))
                                r = t.matmul(
                                    P[4 + h2][par * 64:(par + 1) * 64,
                                              j * 64:(j + 1) * 64],
                                    cfm[h2 * 64:(h2 + 1) * 64, c * 64:(c + 1) * 64],
                                    spu[h2 * 64:(h2 + 1) * 64, 0, c // 2 + par, :],
                                    start=True, stop=False,
                                    tile_position=(h2 * 64, par * 64))
                                r = t.matmul(
                                    P[4 + h2][par * 64:(par + 1) * 64,
                                              j * 64:(j + 1) * 64],
                                    cfm[h2 * 64:(h2 + 1) * 64, c * 64:(c + 1) * 64],
                                    spu[h2 * 64:(h2 + 1) * 64, 1, c // 2, :],
                                    start=False, stop=True,
                                    tile_position=(h2 * 64, par * 64))
                    return r
                emi("tensor", t_y, sTE)
                wait("vector", sTE)
                def v_y(v, _g=g, _b=b, _hp=hp):
                    for h2 in range(2):
                        yob = yosc.ap().rearrange(
                            "p (bb cp h) -> p bb cp h", bb=2, cp=16)[
                            :, _b, _g * 4:(_g + 1) * 4,
                            _hp * 2 + h2].unsqueeze(2).broadcast_to([128, 4, 64])
                        pv = P[4 + h2].ap().rearrange("p (j l) -> p j l", l=64)[
                            :, 0:4, :]
                        v.tensor_tensor(pv, pv, yob, AL.mult)
                    for par in range(2):
                        pr = slice(par * 64, (par + 1) * 64)
                        ydb = ydsc.ap().rearrange(
                            "p (bb cp h) -> p bb cp h", bb=2, cp=16)[
                            pr, _b, _g * 4:(_g + 1) * 4,
                            _hp * 2:_hp * 2 + 2].unsqueeze(3).broadcast_to(
                            [64, 4, 2, 64])
                        v.tensor_tensor(
                            yts[pr, _g * 4:(_g + 1) * 4, :, :],
                            P[2 + par].ap().rearrange("p (j a l) -> p j a l",
                                                      a=2, l=64)[pr, :, :, :],
                            ydb, AL.mult)
                    r = None
                    for h2 in range(2):
                        r = v.tensor_tensor(
                            yts[:, _g * 4:(_g + 1) * 4, h2, :],
                            yts[:, _g * 4:(_g + 1) * 4, h2, :],
                            P[4 + h2].ap().rearrange("p (j l) -> p j l", l=64)[
                                :, 0:4, :],
                            AL.add)
                    return r
                emi("vector", v_y, sVE)
                for pk in (2, 3, 4, 5):
                    self.consumed(P[pk], sVE)
            if getattr(self, "ssd_lim", 9) < 8:
                return
            for half in range(2):
                self.te_bank(PT)
                wait("tensor", sVE)
                def t_yt(t, _h=half):
                    r = None
                    for i in range(8):
                        cp = _h * 8 + i
                        r = t.transpose(PT[:, i * 128:(i + 1) * 128],
                                        yts.ap().rearrange("p cp a l -> p cp (a l)")[
                                            :, cp, :], idbf[:, :])
                    return r
                emi("tensor", t_yt, sTE)
                wait("vector", sTE)
                def v_yt(v, _h=half, _b=b, _hp=hp):
                    r = None
                    for i in range(8):
                        cp = _h * 8 + i
                        o = _b * 2048 + cp * 128
                        r = v.tensor_copy(ya[:, _hp, o:o + 128],
                                          PT[:, i * 128:(i + 1) * 128])
                    return r
                emi("vector", v_yt, sVE)
                self.consumed(PT, sVE)

        if stage >= 4:
            for hp in range(getattr(self, "hp_n", 8)):
                for ri, oc in enumerate([hp, 8 + hp, 16 + hp]):
                    wait("gpsimd", sTE, ip_laststat.get(ri, ip_laststat.get(24, 0)))
                    emi("gpsimd", lambda g, _oc=oc, _b=ri: g_ipw(g, _oc, _b), sW, 16)
                    instat_ready[oc] = sW.v
                    instat_bufof[oc] = ri
                wait("vector", sTE)
                def v_diag(v, _hp=hp):
                    r = None
                    for ri, oc in enumerate([_hp, 8 + _hp, 16 + _hp]):
                        for k in range(4):
                            r = v.tensor_scalar(
                                out=diag[:, ri, k, :], in0=idbf[:, :],
                                scalar1=self.convw[:, oc * 4 + k:oc * 4 + k + 1],
                                scalar2=None, op0=AL.mult)
                    return r
                emi("vector", v_diag, sVE)
                diag_ve = sVE.v
                for b in range(B):
                    ssd_hp(hp, b, diag_ve)
                for ri in range(3):
                    ip_laststat[ri] = self.ip_done_te
        HP_TE = sTE.v
        HP_VE = sVE.v
        hpctx.close()
        tailctx = ExitStack()
        srch = self.sb("srch", [64, 512], F32,
                       actx if (self.dumps and stage < 5) else tailctx)

        if stage >= 3:
            wait("vector", sW, R_ALL_W)
            wait("vector", sTE, HP_TE)
            em("vector", lambda v: v.memset(scol[:, 0:1], -9.0))
            em("vector", lambda v: v.memset(scol[:, 1:2], 0.0))
            masks = [[((i % 8) ^ s) + (i // 8) * 8 for i in range(32)] for s in (4, 2, 1)]
            HV = [slice(0, 32), slice(32, 64)]
            def VF():
                emi("vector", lambda v: v.engine_nop(), sVE)
                wait("vector", sVE)
            for it in range(SEARCH_ITERS):
                for hv in HV:
                    em("vector", lambda v, _h=hv: v.tensor_scalar(
                        out=scol[_h, 2:3], in0=scol[_h, 0:1], scalar1=scol[_h, 1:2],
                        scalar2=0.5, op0=AL.add, op1=AL.mult))
                VF()
                for seg in range(8):
                    for hv in HV:
                        em("vector", lambda v, _h=hv, _s=seg: v.tensor_scalar(
                            out=srch[_h, :], in0=r_all[_h, _s * 512:(_s + 1) * 512],
                            scalar1=scol[_h, 2:3], scalar2=None, op0=AL.is_gt,
                            op1=AL.add, accum_out=scnt[_h, _s:_s + 1]))
                VF()
                for hv in HV:
                    em("vector", lambda v, _h=hv: v.tensor_tensor(
                        scnt[_h, 0:4], scnt[_h, 0:4], scnt[_h, 4:8], AL.add))
                VF()
                for hv in HV:
                    em("vector", lambda v, _h=hv: v.tensor_tensor(
                        scnt[_h, 0:2], scnt[_h, 0:2], scnt[_h, 2:4], AL.add))
                VF()
                for hv in HV:
                    em("vector", lambda v, _h=hv: v.tensor_tensor(
                        scol[_h, 3:4], scnt[_h, 0:1], scnt[_h, 1:2], AL.add))
                VF()
                for m in masks:
                    for hv in HV:
                        em("vector", lambda v, _h=hv, _m=m: v.stream_shuffle(
                            scol[_h, 6:7], scol[_h, 3:4], _m))
                    VF()
                    for hv in HV:
                        em("vector", lambda v, _h=hv: v.tensor_tensor(
                            scol[_h, 3:4], scol[_h, 3:4], scol[_h, 6:7], AL.add))
                    VF()
                for hv in HV:
                    em("vector", lambda v, _h=hv: v.tensor_scalar(
                        out=scol[_h, 5:6], in0=scol[_h, 3:4], scalar1=float(CAP),
                        scalar2=None, op0=AL.is_ge))
                VF()
                for hv in HV:
                    em("vector", lambda v, _h=hv: v.tensor_tensor(
                        scol[_h, 6:7], scol[_h, 2:3], scol[_h, 0:1], AL.subtract))
                VF()
                for hv in HV:
                    em("vector", lambda v, _h=hv: v.tensor_scalar(
                        out=scol[_h, 6:7], in0=scol[_h, 6:7], scalar1=scol[_h, 5:6],
                        scalar2=None, op0=AL.mult))
                VF()
                for hv in HV:
                    em("vector", lambda v, _h=hv: v.tensor_tensor(
                        scol[_h, 0:1], scol[_h, 0:1], scol[_h, 6:7], AL.add))
                for hv in HV:
                    em("vector", lambda v, _h=hv: v.tensor_tensor(
                        scol[_h, 7:8], scol[_h, 1:2], scol[_h, 2:3], AL.subtract))
                VF()
                for hv in HV:
                    em("vector", lambda v, _h=hv: v.tensor_scalar(
                        out=scol[_h, 7:8], in0=scol[_h, 7:8], scalar1=scol[_h, 5:6],
                        scalar2=None, op0=AL.mult))
                VF()
                for hv in HV:
                    em("vector", lambda v, _h=hv: v.tensor_tensor(
                        scol[_h, 1:2], scol[_h, 2:3], scol[_h, 7:8], AL.add))
                VF()
            emi("vector", lambda v: v.engine_nop(), sVE)
            wait("tensor", sVE)
            self.te_bank(PG)
            emi("tensor", lambda t: t.transpose(PG[0:1, 0:64], scol[:, 0:1],
                                                idf32[0:64, 0:64]), sTE)
            wait("vector", sTE)
            emi("vector", lambda v: v.tensor_copy(
                lo8[:, :], PG.ap()[0:1, 0:64].rearrange("p (e c) -> p e c", c=8)[
                    :, :, 0]), sVE)
            self.consumed(PG, sVE)
            wait("tensor", sVE)
            self.te_bank(PG)
            emi("tensor", lambda t: t.matmul(PG[:, 0:8], self.ones1[:, :], lo8[:, :],
                                             start=True, stop=True), sTE)
            wait("vector", sTE)
            emi("vector", lambda v: v.tensor_copy(thr8[:, :], PG[:, 0:8]), sVE)
            self.consumed(PG, sVE)
            self.vfill()
            self.vfill()
            em("vector", lambda v: v.tensor_tensor(
                p3[:, 0, :, :], p3[:, 3, :, :],
                thr8.ap().unsqueeze(1).broadcast_to([128, 32, 8]), AL.is_gt))
            em("vector", lambda v: v.tensor_tensor(rta[:, 0, :], rta[:, 0, :],
                                                   rta[:, 4, :], AL.mult))
            emi("vector", lambda v: v.tensor_tensor(
                self.sc_tm.ap().rearrange("p a b -> p (a b)"), rta[:, 0, :],
                rta[:, 1, :], AL.mult), sVE)
        SCORES_VE = sVE.v
        self.SCORES_VE = SCORES_VE

        if stage >= 5:
            opstat = self.sb("opstat", [128, 8, 8, 128], BF16, tailctx)
            wait("gpsimd", sTE, HP_TE)
            wait("gpsimd", sVE, HP_VE)
            emi("gpsimd", lambda g: g.dma_start(
                out=opstat[:, :, :, :],
                in_=self.opw_d.ap().rearrange("p (oc dc o) -> p oc dc o",
                                              oc=8, dc=8)), sW, 16)
            OP_W = sW.v
            wait("tensor", sW, OP_W)
            for ft in range(16):
                for oc in range(8):
                    pb = P[oc // 2]
                    if oc % 2 == 0:
                        self.te_bank(pb)
                    def t_op(t, _oc=oc, _ft=ft, _pb=pb):
                        r = None
                        for kc in range(8):
                            r = t.matmul(
                                _pb[:, (_oc % 2) * 256:(_oc % 2 + 1) * 256],
                                opstat[:, _oc, kc, :],
                                ya[:, kc, _ft * 256:(_ft + 1) * 256],
                                start=(kc == 0), stop=(kc == 7))
                        return r
                    emi("tensor", t_op, sTE)
                wait("vector", sTE)
                def v_op(v, _ft=ft):
                    r = None
                    for oc in range(8):
                        r = v.tensor_copy(
                            ya[:, oc, _ft * 256:(_ft + 1) * 256],
                            P[oc // 2][:, (oc % 2) * 256:(oc % 2 + 1) * 256])
                    return r
                emi("vector", v_op, sVE)
                for j in range(4):
                    self.consumed(P[j], sVE)
        self.run_block("attn")
        tailctx.close()
        PERS = {"xn", "ya", "sc_tm", "scol", "scnt", "lo8", "thr8"}
        needs_attn = any(t in self.attn_handles and t not in PERS
                         for t in self.dumps)
        if self.stage >= 6 and not needs_attn:
            actx.close()
            self.actx_open = None
        else:
            self.actx_open = actx

    def phase_ffn(self):
        nc = self.nc
        self.new_sched()
        em, emi, wait = self.em, self.emi, self.wait
        sX, sW, sTE, sVE, sSE, sOUT = (self.sX, self.sW, self.sTE, self.sVE,
                                       self.sSE, self.sOUT)
        P, PT, PG = self.P, self.PT, self.PG
        xn, ya = self.xn, self.ya
        self.bank_free = {}

        fctx = ExitStack()
        onesr = self.sb("onesr", [8, 128], BF16, fctx)
        wb1 = self.sb("wb1", [128, 8, 8, 128], BF16, fctx)
        wb2 = self.sb("wb2", [128, 8, 8, 128], BF16, fctx)
        hbuf = self.sb("hbuf", [128, 8, 512], BF16, fctx)
        sb_s = self.sb("sb_s", [128, 512], F32, fctx)
        sstg = self.sb("sstg", [8, 512], BF16, fctx)
        xrl = self.sb("xrl", [128, 2, 1024], F32, fctx)
        outst = self.sb("outst", [128, 2, 1024], F32, fctx)

        emi("gpsimd", lambda g: g.dma_start(
            out=wb1[:, :, :, :],
            in_=self.w1_d[0].rearrange("p (dc fc o) -> p dc fc o",
                                       dc=8, fc=8)), sW, 16)
        w1_rdy = sW.v
        emi("gpsimd", lambda g: g.dma_start(
            out=wb2[:, :, :, :],
            in_=self.w2_d[0].rearrange("p (fc oc o) -> p fc oc o",
                                       fc=8, oc=8)), sW, 16)
        w2_rdy = sW.v

        w2_mm_end = 0
        sbcast_last = 0
        for e in range(8):
            wait("gpsimd", sTE, sbcast_last)
            def g_oh(g, _e=e):
                g.memset(onesr[:, :], 1.0)
                return g.affine_select(out=onesr[:, :], in_=onesr[:, :],
                                       compare_op=AL.is_equal, fill=0.0, base=-_e,
                                       pattern=[[0, 128]], channel_multiplier=1)
            emi("gpsimd", g_oh, sW)
            onesr_rdy = sW.v
            wait("tensor", sW, w1_rdy)
            w1_done_at = None
            for ft in range(8):
                fb = ft * 512
                self.te_bank(PG)
                def t_str(t, _ft=ft):
                    r = None
                    for i in range(4):
                        tt = _ft * 4 + i
                        r = t.transpose(PG[0:8, i * 128:(i + 1) * 128],
                                        self.sc_tm.ap().rearrange("p a b -> p (a b)")[
                                            :, tt * 8:(tt + 1) * 8], self.idf32[:, :])
                    return r
                emi("tensor", t_str, sTE)
                wait("vector", sTE)
                def v_sev(v):
                    r = None
                    for i in range(4):
                        r = v.tensor_copy(sstg[:, i * 128:(i + 1) * 128],
                                          PG[0:8, i * 128:(i + 1) * 128])
                    return r
                emi("vector", v_sev, sVE)
                self.consumed(PG, sVE)
                wait("tensor", sVE)
                wait("tensor", sW, onesr_rdy)
                self.te_bank(PG)
                emi("tensor", lambda t: t.matmul(
                    PG[:, :], onesr[:, :], sstg[:, :],
                    start=True, stop=True), sTE)
                sbcast_last = sTE.v
                wait("vector", sTE)
                emi("vector", lambda v: v.tensor_copy(sb_s[:, :], PG[:, :]), sVE)
                self.consumed(PG, sVE)
                for fc in range(8):
                    ph = P[fc % 2]
                    self.te_bank(ph)
                    def t_w1(t, _fc=fc, _fb=fb, _ph=ph):
                        r = None
                        for dc in range(8):
                            r = t.matmul(_ph[:, :], wb1[:, dc, _fc, :],
                                         xn[:, dc, _fb:_fb + 512],
                                         start=(dc == 0), stop=(dc == 7))
                        return r
                    emi("tensor", t_w1, sTE)
                    wait("scalar", sTE)
                    wait("scalar", sTE, w2_mm_end)
                    emi("scalar", lambda s, _fc=fc, _ph=ph: s.activation(
                        hbuf[:, _fc, :], _ph[:, :], AF.Gelu), sSE)
                    self.consumed(ph, sSE)
                if ft == 7:
                    w1_done_at = sTE.v
                wait("vector", sSE)
                emi("vector", lambda v: v.tensor_tensor(
                    hbuf[:, :, :],
                    hbuf[:, :, :],
                    sb_s.ap().unsqueeze(1).broadcast_to([128, 8, 512]),
                    AL.mult), sVE)
                hprime = sVE.v
                wait("tensor", sW, w2_rdy)
                wait("tensor", sVE, hprime)
                for oc in range(8):
                    py = P[2 + oc % 2]
                    self.te_bank(py)
                    def t_w2(t, _oc=oc, _fb=fb, _py=py):
                        r = None
                        for fc in range(8):
                            r = t.matmul(_py[:, :], wb2[:, fc, _oc, :],
                                         hbuf[:, fc, :],
                                         start=(fc == 0), stop=(fc == 7))
                        return r
                    emi("tensor", t_w2, sTE)
                    wait("vector", sTE)
                    emi("vector", lambda v, _oc=oc, _fb=fb, _py=py: v.tensor_tensor(
                        ya[:, _oc, _fb:_fb + 512], ya[:, _oc, _fb:_fb + 512],
                        _py[:, :], AL.add), sVE)
                    self.consumed(py, sVE)
                w2_mm_end = sTE.v
            w2_done_at = sTE.v
            if e + 1 < 8:
                wait("gpsimd", sTE, w1_done_at)
                emi("gpsimd", lambda g, _e=e: g.dma_start(
                    out=wb1[:, :, :, :],
                    in_=self.w1_d[_e + 1].rearrange(
                        "p (dc fc o) -> p dc fc o", dc=8, fc=8)), sW, 16)
                w1_rdy = sW.v
                wait("gpsimd", sTE, w2_done_at)
                emi("gpsimd", lambda g, _e=e: g.dma_start(
                    out=wb2[:, :, :, :],
                    in_=self.w2_d[_e + 1].rearrange(
                        "p (fc oc o) -> p fc oc o", fc=8, oc=8)), sW, 16)
                w2_rdy = sW.v

        wait("tensor", sVE)
        xrl_free = [0, 0]
        out_free = [0, 0]
        for tt in range(32):
            pp = tt % 2
            wait("sync", sVE, xrl_free[pp])
            emi("sync", lambda s, _pp=pp, _tt=tt: s.dma_start(
                out=xrl[:, _pp, :], in_=self.x_d[_tt * 128:(_tt + 1) * 128, :]), sX, 16)
            self.te_bank(PT)
            def t_ep(t, _tt=tt):
                r = None
                for dc in range(8):
                    r = t.transpose(PT[:, dc * 128:(dc + 1) * 128],
                                    ya[:, dc, _tt * 128:(_tt + 1) * 128],
                                    self.idbf[:, :])
                return r
            emi("tensor", t_ep, sTE)
            wait("vector", sTE)
            wait("vector", sX)
            wait("vector", sOUT, out_free[pp])
            emi("vector", lambda v, _pp=pp: v.tensor_tensor(
                outst[:, _pp, :], PT[:, :], xrl[:, _pp, :], AL.add), sVE)
            self.consumed(PT, sVE)
            xrl_free[pp] = sVE.v
            wait("sync", sVE)
            emi("sync", lambda s, _pp=pp, _tt=tt: s.dma_start(
                out=self.out_d[_tt * 128:(_tt + 1) * 128, :], in_=outst[:, _pp, :]),
                sOUT, 16)
            out_free[pp] = sOUT.v
        wait("sync", sOUT, 16 * 32)
        self.run_block("ffn")
        fctx.close()

    def phase_dumps(self):
        if not self.dumps and self.stage >= 6:
            return
        nc = self.nc
        self.new_sched()
        if self.stage < 6:
            self.emi("gpsimd", lambda g: g.dma_start(
                out=self.out_d.ap(), in_=self.x_d.ap()), self.sOUT, 16)
        for tag in self.dumps:
            h = self.attn_handles.get(tag) if hasattr(self, "attn_handles") else None
            srct = {"xn": self.xn, "ya": self.ya, "sc_tm": self.sc_tm}.get(tag, h)
            if srct is None:
                continue
            ap = srct[tuple([slice(None)] * len(srct.shape))] if hasattr(srct, "shape") \
                else srct
            dd = nc.dram_tensor(f"dbg_{tag}", list(ap.shape), ap.dtype,
                                kind="ExternalOutput")
            self.emi("gpsimd", lambda g, _d=dd, _a=ap: g.dma_start(
                out=_d.ap(), in_=_a), self.sOUT, 16)
        self.wait("gpsimd", self.sOUT)
        self.run_block("dumps")


def build_program(stage=STAGE, dumps=(), hp_n=8, q_n=4):
    nc = bass.Bass("TRN2", target_bir_lowering=False)
    b = Builder(nc, stage, dumps)
    b.hp_n = hp_n
    b.q_n = q_n
    return b.build()


def build_program_lim(stage, dumps, hp_n, q_n, ssd_lim):
    nc = bass.Bass("TRN2", target_bir_lowering=False)
    b = Builder(nc, stage, dumps)
    b.hp_n = hp_n
    b.q_n = q_n
    b.ssd_lim = ssd_lim
    return b.build()


def _prep_weights(in_proj_w, gate_w, conv_w, conv_b, w_base, out_proj_w, w1, w2):
    f32 = np.float32
    Wa = np.zeros((26 * 128, 1024), dtype=f32)
    Wa[:3072] = in_proj_w[:3072]
    Wa[24 * 128:24 * 128 + 16] = in_proj_w[3072:3088]
    Wa[25 * 128:25 * 128 + 8] = gate_w
    ipw = np.ascontiguousarray(
        Wa.reshape(26, 128, 8, 128).transpose(3, 0, 2, 1).reshape(128, 26 * 1024))
    convw = np.ascontiguousarray(
        conv_w.reshape(3072, 4).reshape(24, 128, 4).transpose(1, 0, 2).reshape(128, 96))
    convb = np.ascontiguousarray(conv_b.reshape(24, 128).T)
    wbase = np.ascontiguousarray(w_base.reshape(16, 1))
    opw = np.ascontiguousarray(
        out_proj_w.reshape(8, 128, 8, 128).transpose(3, 0, 2, 1).reshape(128, 8192))
    w1l = np.ascontiguousarray(
        w1.reshape(8, 8, 128, 8, 128).transpose(0, 4, 3, 1, 2).reshape(8, 128, 8192))
    w2l = np.ascontiguousarray(
        w2.reshape(8, 8, 128, 8, 128).transpose(0, 4, 3, 1, 2).reshape(8, 128, 8192))
    return ipw, convw, convb, wbase, opw, w1l, w2l


def kernel(L0_attn_x, L0_attn_in_proj_w, L0_attn_conv_w, L0_attn_conv_b,
           L0_attn_w_base, L0_attn_out_proj_w, L0_ffn_gate_w, L0_ffn_w1, L0_ffn_w2):
    x = np.asarray(L0_attn_x, dtype=np.float32)
    ipw, convw, convb, wbase, opw, w1l, w2l = _prep_weights(
        np.asarray(L0_attn_in_proj_w, dtype=np.float32),
        np.asarray(L0_ffn_gate_w, dtype=np.float32),
        np.asarray(L0_attn_conv_w, dtype=np.float32),
        np.asarray(L0_attn_conv_b, dtype=np.float32),
        np.asarray(L0_attn_w_base, dtype=np.float32),
        np.asarray(L0_attn_out_proj_w, dtype=np.float32),
        np.asarray(L0_ffn_w1, dtype=np.float32),
        np.asarray(L0_ffn_w2, dtype=np.float32))

    nc = build_program(stage=6, dumps=())
    in_maps = []
    for c in range(NCORES):
        xs = np.ascontiguousarray(x[2 * c:2 * c + 2].reshape(NTOK, D))
        in_maps.append(dict(x=xs, ipw=ipw, convw=convw, convb=convb, wbase=wbase,
                            opw=opw, w1=w1l, w2=w2l))
    res = run_bass_kernel_spmd(nc, in_maps, core_ids=list(range(NCORES)))
    out = np.concatenate([r["out"].reshape(2, T, D) for r in res.results], axis=0)
    return out
